# version 19
# baseline (speedup 1.0000x reference)
import numpy as np
import ml_dtypes
import concourse.bass as bass
import concourse.mybir as mybir
from concourse.ap import AP
from concourse.bass_utils import run_bass_kernel_spmd

F32 = mybir.dt.float32
BF16 = mybir.dt.bfloat16
I32 = mybir.dt.int32
ALU = mybir.AluOpType
AF = mybir.ActivationFunctionType
AX = mybir.AxisListType
BF = ml_dtypes.bfloat16

NCORES = 8
TOK = 4096
D = 1024
INNER = 4096
NCH = 32
CS = 32
NSC = 128

ENGINES = ("sync", "vector", "scalar", "tensor", "gpsimd")
SERIAL_ENGINES = ("vector", "scalar", "gpsimd")


class Prog:

    def __init__(self, nc):
        self.nc = nc
        self.items = {e: [] for e in ENGINES}
        self.count = {}
        self.ecount = {e: 0 for e in ENGINES}
        self.streams = set()

    def op(self, eng, fn, deps=(), inc=1, sem=None):
        semname = sem or eng
        if sem is not None:
            self.streams.add(sem)
        waits = [d for d in deps if d is not None]
        self.items[eng].append((fn, waits, inc, semname))
        self.count[semname] = self.count.get(semname, 0) + inc
        if semname == eng:
            self.ecount[eng] += inc
        return (semname, self.count[semname])

    def dma(self, eng, stream, out, in_, deps=(), **kw):
        def fn(e, out=out, in_=in_, kw=kw):
            return e.dma_start(out=out, in_=in_, **kw)
        return self.op(eng, fn, deps, inc=16, sem=stream)

    def barrier(self):
        return [(sn, v) for sn, v in self.count.items() if v > 0]

    def emit(self, block, sems):
        for ename in ENGINES:
            if not self.items[ename]:
                continue

            def mk(ename, items):
                def body(eng):
                    water = {}
                    myec = 0
                    serial = ename in SERIAL_ENGINES
                    for fn, waits, inc, semname in items:
                        if serial and myec > water.get(ename, 0):
                            eng.wait_ge(sems[ename], myec)
                            water[ename] = myec
                        for (de, dv) in waits:
                            if de == ename and dv > myec:
                                raise RuntimeError("forward same-engine dep")
                            if dv > water.get(de, 0):
                                eng.wait_ge(sems[de], dv)
                                water[de] = dv
                        inst = fn(eng)
                        if inc > 0:
                            inst.then_inc(sems[semname], inc)
                        if semname == ename:
                            myec += inc
                return body

            getattr(block, ename)(mk(ename, self.items[ename]))


def apx(t, offset, dims):
    full = t[:]
    pitch = full.ap[0][0]
    nparts = full.ap[0][1]
    return AP(full.tensor, offset, [[pitch, nparts]] + [list(d) for d in dims])


def build_program():
    nc = bass.Bass("TRN2", target_bir_lowering=False)

    x_in = nc.dram_tensor("x", [TOK, D], F32, kind="ExternalInput").ap()
    wgT = nc.dram_tensor("wgT", [D, INNER], F32, kind="ExternalInput").ap()
    wuT = nc.dram_tensor("wuT", [D, INNER], F32, kind="ExternalInput").ap()
    wdT = nc.dram_tensor("wdT", [INNER, D], F32, kind="ExternalInput").ap()
    pw = nc.dram_tensor("pw", [D, 256], F32, kind="ExternalInput").ap()
    pbrow = nc.dram_tensor("pbrow", [1, 256], F32, kind="ExternalInput").ap()
    cwT_in = nc.dram_tensor("cwT", [16, D], F32, kind="ExternalInput").ap()
    drow = nc.dram_tensor("drow", [1, D], F32, kind="ExternalInput").ap()
    rc_in = nc.dram_tensor("rc", [128, NCH * 16], F32, kind="ExternalInput").ap()
    rbh_in = nc.dram_tensor("rbh", [128, NCH * 16], F32, kind="ExternalInput").ap()
    rbl_in = nc.dram_tensor("rbl", [128, NCH * 16], F32, kind="ExternalInput").ap()
    ident_in = nc.dram_tensor("ident", [128, 128], BF16, kind="ExternalInput").ap()
    ones1b_in = nc.dram_tensor("ones1b", [1, 128], BF16, kind="ExternalInput").ap()
    ones1f_in = nc.dram_tensor("ones1f", [1, 128], F32, kind="ExternalInput").ap()
    onescol_in = nc.dram_tensor("onescol", [1, 1], F32, kind="ExternalInput").ap()
    irow_in = nc.dram_tensor("irow", [1, 256], BF16, kind="ExternalInput").ap()
    xinit_in = nc.dram_tensor("xinit", [1, 272], F32, kind="ExternalInput").ap()
    y_out = nc.dram_tensor("y_out", [TOK, D], F32, kind="ExternalOutput").ap()

    d_abar = nc.dram_tensor("d_abar", [TOK, 256], BF16, kind="Internal").ap()
    d_bx = nc.dram_tensor("d_bx", [TOK, 16], F32, kind="Internal").ap()
    d_ss = nc.dram_tensor("d_ss", [TOK, 2], F32, kind="Internal").ap()
    d_hq = nc.dram_tensor("d_hq", [TOK, INNER], BF16, kind="Internal").ap()
    d_wdt = nc.dram_tensor("d_wdt", [INNER, D], BF16, kind="Internal").ap()
    d_htm = nc.dram_tensor("d_htm", [TOK, 16], BF16, kind="Internal").ap()
    d_h0 = nc.dram_tensor("d_h0", [NSC, 16], F32, kind="Internal").ap()
    d_xc = nc.dram_tensor("d_xc", [NSC, 272], F32, kind="Internal").ap()
    d_xsto = nc.dram_tensor("d_xsto", [128, 32 * 272], F32, kind="Internal").ap()
    d_part = nc.dram_tensor("d_part", [TOK, D], F32, kind="Internal").ap()


    P = Prog(nc)
    S, V, A, T, G = "sync", "vector", "scalar", "tensor", "gpsimd"

    from contextlib import ExitStack
    st_top = ExitStack()
    st_A = ExitStack()
    st_pro = ExitStack()
    st_A2 = ExitStack()

    def sbuf(st, name, shape, dt):
        return st.enter_context(nc.sbuf_tensor(name, shape, dt))

    def psum(st, name, shape, dt):
        return st.enter_context(nc.psum_tensor(name, shape, dt))

    c_ident = sbuf(st_top, "c_ident", [128, 128], BF16)
    c_ones1b = sbuf(st_top, "c_ones1b", [1, 128], BF16)
    c_ones1f = sbuf(st_top, "c_ones1f", [1, 128], F32)
    c_onescol = sbuf(st_top, "c_onescol", [128, 1], F32)
    c_rc = sbuf(st_top, "c_rc", [128, NCH * 16], F32)
    c_itile = sbuf(st_top, "c_itile", [128, 256], BF16)
    c_xinit = sbuf(st_top, "c_xinit", [128, 272], F32)
    c_pw = sbuf(st_top, "c_pw", [128, 8 * 256], BF16)
    c_pbrow = sbuf(st_top, "c_pbrow", [1, 256], BF16)
    tabs = {}
    for nm in ["rstd", "beta", "alpha", "betaq", "cg", "cu", "cd"]:
        tabs[nm] = sbuf(st_top, "tab_" + nm, [128, NCH], F32)
    sx = [sbuf(st_top, f"sx{i}", [128, D], F32) for i in range(2)]
    qy = sbuf(st_top, "qy", [128, D], F32)
    qi = sbuf(st_top, "qi", [128, D], I32)
    qm = sbuf(st_top, "qm", [128, D], F32)
    sw = {w: sbuf(st_top, f"sw_{w}", [128, 1], F32) for w in ["g", "u", "d"]}
    t1c = sbuf(st_top, "t1c", [128, 1], F32)
    t2c = sbuf(st_top, "t2c", [128, 1], F32)
    t3c = sbuf(st_top, "t3c", [128, 1], F32)
    swg_r = sbuf(st_top, "swg_r", [128, 1], F32)
    swu_r = sbuf(st_top, "swu_r", [128, 1], F32)
    bnst = sbuf(st_top, "bnst", [128, 12], F32)
    bnag = sbuf(st_top, "bnag", [128, 2], F32)

    p_tra = psum(st_top, "p_tra", [128, 1024], BF16)
    p_trb = psum(st_top, "p_trb", [128, 1024], BF16)

    c_rb = sbuf(st_top, "c_rb", [128, NCH * 16], F32)
    c_msk = sbuf(st_top, "c_msk", [128, 1], F32)

    c_half = sbuf(st_top, "c_half", [128, 1], F32)
    c_eps = sbuf(st_top, "c_eps", [128, 1], F32)
    P.op(V, lambda v: v.memset(c_half[:], 0.5))
    P.op(V, lambda v: v.memset(c_eps[:], 1e-5))
    nc.const_aps.register(F32, 0.5, c_half[:, 0:1])
    nc.const_aps.register(F32, 1e-5, c_eps[:, 0:1])

    P.dma(S, "cst", c_ident[:], ident_in)
    P.dma(S, "cst", c_ones1b[:], ones1b_in)
    P.dma(S, "cst", c_ones1f[:], ones1f_in)
    P.dma(S, "cst", c_onescol[:], onescol_in.partition_broadcast(128))
    P.dma(S, "cst", c_rc[:], rc_in)
    P.dma(S, "cst", c_rb[:], rbh_in)
    P.dma(S, "cst", qy[:, 0:NCH * 16], rbl_in)
    P.dma(S, "cst", c_itile[:], irow_in.partition_broadcast(128))
    f_cst = P.dma(S, "cst", c_xinit[:], xinit_in.partition_broadcast(128))
    f_ident = f_ones1b = f_ones1f = f_onescol = f_itile = f_xinit = f_cst
    fm1 = P.op(V, lambda v: v.memset(c_msk[:], 1.0), deps=[f_cst])
    fm2 = P.op(V, lambda v: v.tensor_scalar(
        out=c_msk[:], in0=c_msk[:], scalar1=2.0 ** -30, scalar2=-1.0,
        op0=ALU.add, op1=ALU.add), deps=[fm1])
    fm3 = P.op(V, lambda v: v.tensor_scalar(
        out=c_msk[:], in0=c_msk[:], scalar1=0.0, scalar2=None, op0=ALU.is_gt),
        deps=[fm2])
    fm4 = P.op(V, lambda v: v.tensor_tensor(
        out=c_rb[:], in0=c_rb[:], in1=qy[:, 0:NCH * 16], op=ALU.add), deps=[fm3])
    fm5 = P.op(V, lambda v: v.scalar_tensor_tensor(
        out=c_rb[:], in0=c_rb[:], scalar=c_msk[:], in1=c_rc[:],
        op0=ALU.mult, op1=ALU.subtract), deps=[fm4])
    f_rc = P.op(V, lambda v: v.scalar_tensor_tensor(
        out=c_rc[:], in0=c_rb[:], scalar=c_msk[:], in1=c_rc[:],
        op0=ALU.mult, op1=ALU.add), deps=[fm5])
    pw_src = AP(pw.tensor, 0, [[256, 128], [128 * 256, 8], [1, 256]])
    P.dma(G, "gcst", apx(c_pw, 0, [[256, 8], [1, 256]]), pw_src)
    f_gcst = P.dma(G, "gcst", c_pbrow[:], pbrow)
    f_pw = f_pb = f_gcst

    wg_t = sbuf(st_A, "wg_t", [128, 8 * 4096], BF16)
    wu_t = sbuf(st_A, "wu_t", [128, 8 * 4096], BF16)
    wtmp = [sbuf(st_pro, f"wtmp{i}", [128, 4096], F32) for i in range(2)]
    wabs = sbuf(st_pro, "wabs", [128, 32], F32)
    wsum1 = sbuf(st_pro, "wsum1", [1, 1], F32)
    wd4 = sbuf(st_pro, "wd4", [128, 1024], BF16)
    p_sum = psum(st_pro, "p_sum", [128, 512], F32)

    tstate = {"buf_free": [None, None], "prev_ea": None, "prev_ua": None,
              "prev_dma": None, "pre_c": 0, "prev_f": None}

    def tern_pass1(src_ap, n_kt, kw, sw_col, extra_dep=None):
        reds = []
        bf = tstate["buf_free"]
        for k in range(n_kt):
            wb = wtmp[k % 2]
            fl = P.dma(S if k % 2 == 0 else G, "wt%d" % (k % 2), wb[:, 0:kw],
                       src_ap[k * 128:(k + 1) * 128, :], deps=[bf[k % 2]])
            fr = P.op(V, lambda v, wb=wb, k=k, kw=kw: v.tensor_reduce(
                out=wabs[:, k:k + 1], in_=wb[:, 0:kw],
                axis=AX.X, op=ALU.add, apply_absolute_value=True), deps=[fl])
            bf[k % 2] = fr
            reds.append(fr)
        fs8 = P.op(V, lambda v, n_kt=n_kt: v.tensor_reduce(
            out=wabs[:, 0:1], in_=wabs[:, 0:n_kt], axis=AX.X, op=ALU.add),
            deps=reds)
        fmm = P.op(T, lambda t: t.matmul(p_sum[0:1, 0:1], wabs[:, 0:1],
                                         c_onescol[:], start=True, stop=True),
                   deps=[fs8, f_onescol, extra_dep])
        fev = P.op(V, lambda v: v.tensor_copy(wsum1[:], p_sum[0:1, 0:1]),
                   deps=[fmm])
        fbc = P.op(T, lambda t: t.matmul(p_sum[:, 0:1],
                                         apx(c_ones1f, 0, [[1, 128]]), wsum1[:],
                                         start=True, stop=True),
                   deps=[fev, f_ones1f])
        n_el = n_kt * 128 * kw
        f1 = P.op(V, lambda v, n_el=n_el: v.tensor_scalar(
            out=sw_col[:], in0=p_sum[:, 0:1], scalar1=1.0 / n_el, scalar2=1e-5,
            op0=ALU.mult, op1=ALU.max), deps=[fbc])
        return P.op(V, lambda v: v.reciprocal(out=sw_col[:], in_=sw_col[:]),
                    deps=[f1])

    def emit_prepass(c):
        b = c % 2
        fx = P.dma(A, "xt%d" % b, sx[b][:], x_in[c * 128:(c + 1) * 128, :],
                   deps=[pre_done[c - 2] if c >= 2 else None])
        f1 = P.op(V, lambda v, b=b: v.bn_stats(bnst[:, 0:6], sx[b][:, 0:512]),
                  deps=[fx, tstate["prev_f"]])
        f2 = P.op(V, lambda v, b=b: v.bn_stats(bnst[:, 6:12], sx[b][:, 512:1024]),
                  deps=[f1])
        f3 = P.op(V, lambda v: v.bn_aggr(bnag[:], bnst[:]), deps=[f2])
        f4 = P.op(A, lambda s, c=c: s.activation(
            out=tabs["rstd"][:, c:c + 1], in_=bnag[:, 1:2], func=AF.Ln, bias=1e-5),
            deps=[f3])
        f5 = P.op(A, lambda s, c=c: s.activation(
            out=tabs["rstd"][:, c:c + 1], in_=tabs["rstd"][:, c:c + 1], func=AF.Exp,
            scale=-0.5), deps=[f4])
        fmx = P.op(V, lambda v, b=b: v.tensor_reduce(
            out=t1c[:], in_=sx[b][:], axis=AX.X, op=ALU.max), deps=[fx, f3])
        fmn = P.op(V, lambda v, b=b: v.tensor_reduce(
            out=t3c[:], in_=sx[b][:], axis=AX.X, op=ALU.min), deps=[fmx])
        fz1 = P.op(V, lambda v: v.tensor_scalar(
            out=t1c[:], in0=t1c[:], scalar1=bnag[:, 0:1], scalar2=None,
            op0=ALU.subtract), deps=[fmx, f3])
        fz2 = P.op(V, lambda v: v.scalar_tensor_tensor(
            out=t3c[:], in0=t3c[:], scalar=-1.0, in1=bnag[:, 0:1],
            op0=ALU.mult, op1=ALU.add), deps=[fmn])
        f7 = P.op(V, lambda v: v.tensor_tensor(
            out=t1c[:], in0=t1c[:], in1=t3c[:], op=ALU.max), deps=[fz1, fz2])
        f8 = P.op(V, lambda v, c=c: v.tensor_tensor(
            out=t1c[:], in0=t1c[:], in1=tabs["rstd"][:, c:c + 1], op=ALU.mult),
            deps=[f7, f5])
        f8b = P.op(V, lambda v: v.tensor_scalar(
            out=t1c[:], in0=t1c[:], scalar1=1e-5, scalar2=None, op0=ALU.max),
            deps=[f8])
        f9 = P.op(V, lambda v: v.reciprocal(out=t2c[:], in_=t1c[:]), deps=[f8b])
        f10 = P.op(V, lambda v, c=c: v.scalar_tensor_tensor(
            out=tabs["alpha"][:, c:c + 1], in0=t2c[:], scalar=127.0,
            in1=tabs["rstd"][:, c:c + 1], op0=ALU.mult, op1=ALU.mult), deps=[f9])
        f11 = P.op(V, lambda v, c=c: v.scalar_tensor_tensor(
            out=tabs["beta"][:, c:c + 1], in0=bnag[:, 0:1], scalar=-1.0,
            in1=tabs["rstd"][:, c:c + 1], op0=ALU.mult, op1=ALU.mult), deps=[f3])
        f12 = P.op(V, lambda v, c=c: v.scalar_tensor_tensor(
            out=tabs["betaq"][:, c:c + 1], in0=bnag[:, 0:1], scalar=-1.0,
            in1=tabs["alpha"][:, c:c + 1], op0=ALU.mult, op1=ALU.mult), deps=[f10])
        f12b = P.op(V, lambda v, c=c: v.tensor_scalar(
            out=tabs["betaq"][:, c:c + 1], in0=tabs["betaq"][:, c:c + 1],
            scalar1=0.5, scalar2=None, op0=ALU.add), deps=[f12])
        f14 = P.op(V, lambda v, c=c: v.scalar_tensor_tensor(
            out=tabs["cg"][:, c:c + 1], in0=swg_r[:], scalar=1.0 / 127.0,
            in1=t1c[:], op0=ALU.mult, op1=ALU.mult), deps=[f_rswg, f12b])
        f16 = P.op(V, lambda v, c=c: v.scalar_tensor_tensor(
            out=tabs["cu"][:, c:c + 1], in0=swu_r[:], scalar=1.0 / 127.0,
            in1=t1c[:], op0=ALU.mult, op1=ALU.mult), deps=[f_rswu])
        tstate["prev_f"] = f16
        pre_done.append(f16)

    def tern_pass2(src_ap, n_kt, kw, dst_fn, sw_col, sw_dep, dram_out=None,
                   hook=None):
        tile_done = []
        bf = tstate["buf_free"]
        for k in range(n_kt):
            wb = wtmp[k % 2]
            fl = P.dma(S if k % 2 == 0 else G, "wt%d" % (k % 2), wb[:, 0:kw],
                       src_ap[k * 128:(k + 1) * 128, :], deps=[bf[k % 2]])
            subs = []
            wsub = 1024
            for s4 in range(kw // wsub):
                wsl = wb[:, s4 * wsub:(s4 + 1) * wsub]
                ya = P.op(A, lambda s, w=wsl: s.activation(
                    out=w, in_=w, func=AF.Identity, bias=0.5, scale=sw_col[:]),
                    deps=[fl, sw_dep])
                ia = P.op(G, lambda g, w=wsl, n=wsub: g.tensor_copy(
                    qi[:, 0:n], w), deps=[ya, tstate["prev_ua"]])
                ma = P.op(G, lambda g, w=wsl, n=wsub: g.tensor_tensor(
                    out=qm[:, 0:n], in0=w, in1=qi[:, 0:n], op=ALU.subtract),
                    deps=[ia, tstate["prev_ea"]])
                ua = P.op(V, lambda v, n=wsub: v.scalar_tensor_tensor(
                    out=qy[:, 0:n], in0=qm[:, 0:n], scalar=1.0,
                    in1=qi[:, 0:n], op0=ALU.is_ge, op1=ALU.add), deps=[ma])
                tstate["prev_ua"] = ua
                ua = P.op(V, lambda v, n=wsub: v.scalar_tensor_tensor(
                    out=qy[:, 0:n], in0=qm[:, 0:n], scalar=-1.0,
                    in1=qy[:, 0:n], op0=ALU.is_lt, op1=ALU.subtract),
                    deps=[ua])
                ea = P.op(V, lambda v, w=wsl, n=wsub: v.scalar_tensor_tensor(
                    out=w, in0=qm[:, 0:n], scalar=0.0, in1=qy[:, 0:n],
                    op0=ALU.is_lt, op1=ALU.add), deps=[ua])
                tstate["prev_ea"] = ea
                if dram_out is None:
                    ca = P.op(V, lambda v, k=k, s4=s4, w=wsl, n=wsub: v.tensor_scalar(
                        out=dst_fn(k)[:, s4 * n:(s4 + 1) * n], in0=w,
                        scalar1=-1.0, scalar2=1.0, op0=ALU.max, op1=ALU.min),
                        deps=[ea])
                else:
                    ca = P.op(V, lambda v, w=wsl: v.tensor_scalar(
                        out=wd4[:], in0=w, scalar1=-1.0, scalar2=1.0,
                        op0=ALU.max, op1=ALU.min), deps=[ea, tstate["prev_dma"]])
                    tstate["prev_dma"] = P.dma(
                        S, "wdo", dram_out[k * 128:(k + 1) * 128, :],
                        wd4[:], deps=[ca])
                    ca = tstate["prev_dma"]
                subs.append(ca)
            bf[k % 2] = subs[-1]
            tile_done.append(subs[-1])
            if hook is not None:
                for _ in range(2):
                    if tstate["pre_c"] < NCH:
                        emit_prepass(tstate["pre_c"])
                        tstate["pre_c"] += 1
        return tile_done

    pre_done = []
    f_swg = tern_pass1(wgT, 8, 4096, sw["g"])
    f_swu = tern_pass1(wuT, 8, 4096, sw["u"], extra_dep=f_swg)
    f_rswg = P.op(V, lambda v: v.reciprocal(out=swg_r[:], in_=sw["g"][:]),
                  deps=[f_swg])
    f_rswu = P.op(V, lambda v: v.reciprocal(out=swu_r[:], in_=sw["u"][:]),
                  deps=[f_swu])
    fin_g = tern_pass2(wgT, 8, 4096,
                       lambda k: wg_t[:, k * 4096:(k + 1) * 4096], sw["g"],
                       f_swg, hook=True)
    fin_u = tern_pass2(wuT, 8, 4096,
                       lambda k: wu_t[:, k * 4096:(k + 1) * 4096], sw["u"],
                       f_swu, hook=True)
    f_swd = tern_pass1(wdT, 32, 1024, sw["d"], extra_dep=f_swu)
    fin_d = tern_pass2(wdT, 32, 1024, None, sw["d"], f_swd, dram_out=d_wdt)
    while tstate["pre_c"] < NCH:
        emit_prepass(tstate["pre_c"])
        tstate["pre_c"] += 1

    st_pro.close()
    prologue_done = P.barrier()

    qhy = sbuf(st_A2, "qhy", [128, 1024], F32)
    qhi = sbuf(st_A2, "qhi", [128, 1024], I32)
    qhm = sbuf(st_A2, "qhm", [128, 1024], F32)
    xn_tm = sbuf(st_A2, "xn_tm", [128, D], BF16)
    xq_tm = xn_tm
    xn_fm = sbuf(st_A2, "xn_fm", [128, D], BF16)
    xq_fm = sbuf(st_A2, "xq_fm", [128, D], BF16)
    g_sb = sbuf(st_A2, "g_sb", [128, INNER], BF16)
    u_sb = sbuf(st_A2, "u_sb", [128, INNER], BF16)
    hu_sb = sbuf(st_A2, "hu_sb", [128, INNER], BF16)
    hq_q = sbuf(st_A2, "hq_q", [128, 1024], BF16)
    gam = sbuf(st_A2, "gam", [128, 1], F32)
    vp_sb = sbuf(st_A2, "vp_sb", [128, 240], BF16)
    bx_sb = sbuf(st_A2, "bx_sb", [128, 16], F32)
    a_tm = sbuf(st_A2, "a_tm", [128, 512], BF16)
    pvec = sbuf(st_A2, "pvec", [128, 32], F32)
    sstab = sbuf(st_A2, "sstab", [128, 2], F32)
    mv_scr = sbuf(st_A2, "mv_scr", [128, 256], F32)

    p_v = psum(st_A2, "p_v", [128, 512], F32)
    p_g = [psum(st_A2, f"p_g{i}", [128, 512], F32) for i in range(4)]

    f_a0 = P.op(V, lambda v: v.memset(a_tm[:], 0.0), deps=prologue_done)
    swd_r = sbuf(st_A2, "swd_r", [128, 1], F32)
    f_swr = P.op(V, lambda v: v.reciprocal(out=swd_r[:], in_=sw["d"][:]),
                 deps=[fin_d[-1]])

    apit = a_tm[:].ap[0][0]
    vpit = vp_sb[:].ap[0][0]
    mpit = mv_scr[:].ap[0][0]
    ppit = pvec[:].ap[0][0]
    rpit = c_rc[:].ap[0][0]

    slot_x = [None, None]
    prev_fxq = [None]
    prev_fya = [None]
    prev_qh = [None, None]
    pending_q = [None]
    prev_fh = [None]
    prev_mxg = [None]

    hq_free = None
    psg_free = [None, None, None, None]
    ptr_free = {0: None, 1: None}
    prev_xnT = None
    prev_xqT = None
    prev_vb = None
    prev_mx = None
    a_fhq = []
    for c in range(NCH):
        b = c % 2
        fx = P.dma(G, "xg%d" % b, sx[b][:], x_in[c * 128:(c + 1) * 128, :],
                   deps=[slot_x[b] if slot_x[b] is not None else pre_done[-1]])
        fxn = P.op(V, lambda v, b=b, c=c: v.tensor_scalar(
            out=xn_tm[:], in0=sx[b][:], scalar1=tabs["rstd"][:, c:c + 1],
            scalar2=tabs["beta"][:, c:c + 1], op0=ALU.mult, op1=ALU.add),
            deps=[fx, pre_done[c], prev_xnT])
        prevq = [prev_fxq[0]] if prev_fxq[0] else []
        fyq = P.op(V, lambda v, b=b, c=c: v.tensor_scalar(
            out=qy[:], in0=sx[b][:], scalar1=tabs["alpha"][:, c:c + 1],
            scalar2=tabs["betaq"][:, c:c + 1], op0=ALU.mult, op1=ALU.add),
            deps=[fx, pre_done[c]] + prevq)
        slot_x[b] = fyq
        fi = P.op(V, lambda v: v.tensor_copy(qi[:], qy[:]), deps=[fyq])
        fm = P.op(V, lambda v: v.tensor_tensor(out=qm[:], in0=qy[:], in1=qi[:],
                                               op=ALU.subtract), deps=[fi])
        fu = P.op(V, lambda v: v.scalar_tensor_tensor(
            out=qy[:], in0=qm[:], scalar=1.0, in1=qi[:], op0=ALU.is_ge, op1=ALU.add),
            deps=[fm])
        fu = P.op(V, lambda v: v.scalar_tensor_tensor(
            out=qy[:], in0=qm[:], scalar=-1.0, in1=qy[:],
            op0=ALU.is_lt, op1=ALU.subtract), deps=[fu])
        fxq_dep = [fu, prev_xqT]

        ft = None
        for k in range(8):
            ft = P.op(T, lambda t, k=k: t.transpose(
                p_tra[:, k * 128:(k + 1) * 128], xn_tm[:, k * 128:(k + 1) * 128],
                c_ident[:]),
                deps=[fxn, f_ident] + ([ptr_free[0]] if k == 0 else []))
        fe_xn = P.op(A, lambda s: s.copy(xn_fm[:], p_tra[:, 0:1024]), deps=[ft])
        ptr_free[0] = fe_xn
        fxq = P.op(V, lambda v: v.scalar_tensor_tensor(
            out=xq_tm[:], in0=qm[:], scalar=0.0, in1=qy[:],
            op0=ALU.is_lt, op1=ALU.add), deps=fxq_dep + [ft])
        prev_fxq[0] = fxq
        for k in range(8):
            ft = P.op(T, lambda t, k=k: t.transpose(
                p_trb[:, k * 128:(k + 1) * 128], xq_tm[:, k * 128:(k + 1) * 128],
                c_ident[:]),
                deps=[fxq] + ([ptr_free[1]] if k == 0 else []))
        fe_xq = P.op(A, lambda s: s.copy(xq_fm[:], p_trb[:, 0:1024]), deps=[ft])
        ptr_free[1] = fe_xq
        prev_xnT = ft
        ev_xn = [fe_xn] * 8
        ev_xq = [fe_xq] * 8
        prev_xqT = fe_xq

        if pending_q[0] is not None:
            pending_q[0]()
            pending_q[0] = None

        fb0 = P.op(T, lambda t: t.matmul(p_v[:, 0:256], c_ones1b[:], c_pbrow[:],
                                         start=True, stop=False),
                   deps=[f_ones1b, f_pb, prev_vb])
        fvm = fb0
        for k in range(8):
            fvm = P.op(T, lambda t, k=k: t.matmul(
                p_v[:, 0:256], xn_fm[:, k * 128:(k + 1) * 128],
                c_pw[:, k * 256:(k + 1) * 256], start=False, stop=(k == 7)),
                deps=[ev_xn[k]] + ([f_pw] if k == 0 else []))
        fve = P.op(A, lambda a: a.copy(vp_sb[:], p_v[:, 0:240]), deps=[fvm])
        fbe = P.op(A, lambda a: a.copy(bx_sb[:], p_v[:, 240:256]),
                   deps=[fvm, prev_mx])
        prev_vb = fbe
        fbxd = P.dma(S, "mxs", d_bx[c * 128:(c + 1) * 128, :], bx_sb[:], deps=[fbe])
        vpU = apx(vp_sb, 0, [[16, 15], [1, 16]])
        aU = apx(a_tm, 1, [[1, 15], [17, 16]])
        aL = apx(a_tm, 16, [[16, 15], [17, 16]])
        fa1 = P.op(G, lambda g: g.tensor_copy(aU, vpU), deps=[fve, f_a0, prev_mx])
        fa2 = P.op(G, lambda g: g.tensor_scalar(
            out=aL, in0=vpU, scalar1=-1.0, scalar2=None, op0=ALU.mult), deps=[fa1])
        in0A = AP(a_tm[:].tensor, 0, [[apit, 128], [16, 16], [1, 16]])
        fs3 = fs4 = None
        fprev = [fa2, f_rc]
        for it in range(4):
            if it == 0:
                in1p = AP(c_rc[:].tensor, c * 16, [[rpit, 128], [0, 16], [1, 16]])
            else:
                in1p = AP(pvec[:].tensor, ((it - 1) % 2) * 16,
                          [[ppit, 128], [0, 16], [1, 16]])
            dst = pvec[:, (it % 2) * 16:(it % 2) * 16 + 16]
            fmul = P.op(V, lambda v, in1p=in1p: v.tensor_tensor(
                out=mv_scr[:], in0=in0A, in1=in1p, op=ALU.mult), deps=fprev)
            mv3 = AP(mv_scr[:].tensor, 0, [[mpit, 128], [16, 16], [1, 16]])
            fred = P.op(V, lambda v, dst=dst: v.tensor_reduce(
                out=dst, in_=mv3, axis=AX.X, op=ALU.add), deps=[fmul])
            if it >= 2:
                fsm = P.op(V, lambda v, dst=dst: v.tensor_tensor(
                    out=mv_scr[:, 0:16], in0=dst, in1=dst, op=ALU.mult), deps=[fred])
                fss_ = P.op(V, lambda v, it=it: v.tensor_reduce(
                    out=sstab[:, it - 2:it - 1], in_=mv_scr[:, 0:16], axis=AX.X,
                    op=ALU.add), deps=[fsm, prev_mx])
                if it == 2:
                    fs3 = fss_
                else:
                    fs4 = fss_
            fprev = [fred]
        fssd = P.dma(S, "mxs", d_ss[c * 128:(c + 1) * 128, :], sstab[:],
                     deps=[fs3, fs4])
        fad = P.dma(S, "mxs", d_abar[c * 128:(c + 1) * 128, :], a_tm[:, 0:256],
                    deps=[fa2, fs4])
        prev_mx = fad

        ep_g = [None] * 8
        ep_u = [None] * 8
        for grp in range(4):
            slices = list(range(grp * 4, grp * 4 + 4))
            mm_last = [None] * 4
            for k in range(8):
                for j, sl in enumerate(slices):
                    if sl < 8:
                        w_ap = wg_t[:, k * 4096 + sl * 512: k * 4096 + sl * 512 + 512]
                        wdep = fin_g[k]
                    else:
                        w_ap = wu_t[:, k * 4096 + (sl - 8) * 512:
                                    k * 4096 + (sl - 8) * 512 + 512]
                        wdep = fin_u[k]
                    deps = []
                    if j == 0:
                        deps.append(ev_xq[k])
                    if k == 0:
                        deps.append(wdep)
                        deps.append(psg_free[j])
                    mm_last[j] = P.op(T, lambda t, k=k, j=j, w=w_ap: t.matmul(
                        p_g[j][:, 0:512], xq_fm[:, k * 128:(k + 1) * 128], w,
                        start=(k == 0), stop=(k == 7)), deps=deps)
            for j, sl in enumerate(slices):
                if sl < 8:
                    fep = P.op(A, lambda s, j=j, sl=sl, c=c: s.activation(
                        out=g_sb[:, sl * 512:(sl + 1) * 512], in_=p_g[j][:, 0:512],
                        func=AF.Silu, scale=tabs["cg"][:, c:c + 1]),
                        deps=[mm_last[j], prev_fh[0]])
                    ep_g[sl] = fep
                    psg_free[j] = fep
                else:
                    su = sl - 8
                    feu = P.op(A, lambda s, j=j, su=su: s.copy(
                        u_sb[:, su * 512:(su + 1) * 512], p_g[j][:, 0:512]),
                        deps=[mm_last[j], prev_fh[0]])
                    ep_u[su] = feu
                    psg_free[j] = feu
        def emit_hquant(c=c, ep_g=ep_g, ep_u=ep_u):
            nonlocal hq_free
            fh_full = P.op(V, lambda v: v.tensor_tensor(
                out=hu_sb[:], in0=g_sb[:], in1=u_sb[:], op=ALU.mult),
                deps=ep_g + ep_u + [prev_fya[0]])
            prev_fh[0] = fh_full
            fgm = P.op(V, lambda v: v.tensor_reduce(
                out=gam[:], in_=hu_sb[:], axis=AX.X, op=ALU.max,
                apply_absolute_value=True), deps=[fh_full])
            prev_mxg[0] = fgm
            fg1 = P.op(V, lambda v, c=c: v.tensor_tensor(
                out=gam[:], in0=gam[:], in1=tabs["cu"][:, c:c + 1], op=ALU.mult),
                deps=[fgm])
            fg2 = P.op(V, lambda v: v.tensor_scalar(
                out=gam[:], in0=gam[:], scalar1=1e-5, scalar2=None, op0=ALU.max),
                deps=[fg1])
            fg4 = P.op(V, lambda v, c=c: v.scalar_tensor_tensor(
                out=tabs["cd"][:, c:c + 1], in0=swd_r[:], scalar=1.0 / 127.0,
                in1=gam[:], op0=ALU.mult, op1=ALU.mult), deps=[fg2, f_swr])
            fg7 = P.op(V, lambda v: v.reciprocal(out=t1c[:], in_=gam[:]),
                       deps=[fg4])
            fg8 = P.op(V, lambda v, c=c: v.scalar_tensor_tensor(
                out=gam[:], in0=t1c[:], scalar=127.0, in1=tabs["cu"][:, c:c + 1],
                op0=ALU.mult, op1=ALU.mult), deps=[fg7])
            fq_dma = None
            fqa_p, fma_p = prev_qh
            fua_p = fqa_p
            for hh in range(4):
                hsl = slice(hh * 1024, (hh + 1) * 1024)
                fya = P.op(V, lambda v, hsl=hsl, fg8=fg8, fh_full=fh_full:
                    v.tensor_scalar(
                        out=qhy[:], in0=hu_sb[:, hsl], scalar1=gam[:, 0:1],
                        scalar2=0.5, op0=ALU.mult, op1=ALU.add),
                    deps=[fg8, fh_full, fqa_p, fma_p])
                prev_fya[0] = fya
                fia = P.op(V, lambda v: v.tensor_copy(qhi[:], qhy[:]),
                           deps=[fya, fua_p])
                fma = P.op(V, lambda v: v.tensor_tensor(
                    out=qhm[:], in0=qhy[:], in1=qhi[:], op=ALU.subtract),
                    deps=[fia, fqa_p])
                fma_p = fma
                fua = P.op(V, lambda v: v.scalar_tensor_tensor(
                    out=qhy[:], in0=qhm[:], scalar=1.0, in1=qhi[:],
                    op0=ALU.is_ge, op1=ALU.add), deps=[fma])
                fua_p = fua
                fub = P.op(V, lambda v: v.scalar_tensor_tensor(
                    out=qhy[:], in0=qhm[:], scalar=-1.0, in1=qhy[:],
                    op0=ALU.is_lt, op1=ALU.subtract), deps=[fua])
                fqa = P.op(V, lambda v: v.scalar_tensor_tensor(
                    out=hq_q[:], in0=qhm[:], scalar=0.0, in1=qhy[:],
                    op0=ALU.is_lt, op1=ALU.add),
                    deps=[fub, hq_free if hh == 0 else fq_dma])
                fqa_p = fqa
                fq_dma = P.dma(S, "hqs",
                               d_hq[c * 128:(c + 1) * 128,
                                    hh * 1024:(hh + 1) * 1024],
                               hq_q[:], deps=[fqa])
            prev_qh[0], prev_qh[1] = fqa_p, fma_p
            hq_free = fq_dma
            a_fhq.append(fq_dma)
        pending_q[0] = emit_hquant

    if pending_q[0] is not None:
        pending_q[0]()
        pending_q[0] = None

    st_A2.close()
    st_A.close()
    barrier_A = P.barrier()

    st_M = ExitStack()
    a_ck = sbuf(st_M, "a_ck", [128, NCH * 256], BF16)
    b_ck = sbuf(st_M, "b_ck", [128, CS * 16], F32)
    ss_ck = sbuf(st_M, "ss_ck", [128, CS * 2], F32)
    s_ck = sbuf(st_M, "s_ck", [128, CS], F32)
    w_ck = sbuf(st_M, "w_ck", [128, CS * 256], BF16)
    bsq = sbuf(st_M, "bsq", [128, 256], BF16)
    e_t = sbuf(st_M, "e_t", [128, 256], BF16)
    f_t = sbuf(st_M, "f_t", [128, 256], BF16)
    prod = sbuf(st_M, "prod", [128, 8192], BF16)
    pfx = sbuf(st_M, "pfx", [128, 4352], BF16)
    xcar = sbuf(st_M, "xcar", [128, 272], F32)
    h0ck = sbuf(st_M, "h0ck", [128, 16], F32)
    hall = sbuf(st_M, "hall", [128, CS * 16], F32)
    comb = sbuf(st_M, "comb", [2, 8 * 272], F32)
    combh = sbuf(st_M, "combh", [2, 48], F32)
    combs = sbuf(st_M, "combs", [2, 256], F32)
    h0sq = sbuf(st_M, "h0sq", [2, 64 * 16], F32)

    apck = a_ck[:].ap[0][0]
    spck = s_ck[:].ap[0][0]
    ppck = prod[:].ap[0][0]
    wpck = w_ck[:].ap[0][0]
    xpit = xcar[:].ap[0][0]
    fpit = pfx[:].ap[0][0]
    cpit = comb[:].ap[0][0]
    hpit = combh[:].ap[0][0]
    h0pit = h0ck[:].ap[0][0]
    ept = e_t[:].ap[0][0]
    fpt = f_t[:].ap[0][0]

    P.dma(S, "msc", a_ck[:], AP(d_abar.tensor, 0, [[CS * 256, 128], [1, CS * 256]]),
          deps=barrier_A)
    P.dma(S, "msc", b_ck[:], AP(d_bx.tensor, 0, [[CS * 16, 128], [1, CS * 16]]),
          deps=barrier_A)
    fl_ss = P.dma(S, "msc", ss_ck[:], AP(d_ss.tensor, 0, [[CS * 2, 128], [1, CS * 2]]),
                  deps=barrier_A)
    fl_a = fl_b = fl_ss

    f_s1 = P.op(V, lambda v: v.tensor_scalar(
        out=s_ck[:], in0=apx(ss_ck, 0, [[2, CS]]), scalar1=1e-30, scalar2=None,
        op0=ALU.max), deps=[fl_ss, barrier_A[0]])
    f_s2 = P.op(V, lambda v: v.reciprocal(out=s_ck[:], in_=s_ck[:]), deps=[f_s1])
    f_s3 = P.op(V, lambda v: v.tensor_tensor(
        out=s_ck[:], in0=s_ck[:], in1=apx(ss_ck, 1, [[2, CS]]), op=ALU.mult),
        deps=[f_s2])
    f_s4 = P.op(V, lambda v: v.tensor_scalar(
        out=s_ck[:], in0=s_ck[:], scalar1=1.0 / 0.09, scalar2=1.0,
        op0=ALU.mult, op1=ALU.max), deps=[f_s3])
    f_s5 = P.op(A, lambda s: s.activation(out=s_ck[:], in_=s_ck[:], func=AF.Ln),
                deps=[f_s4])
    f_s6 = P.op(A, lambda s: s.activation(out=s_ck[:], in_=s_ck[:], func=AF.Exp,
                                          scale=-0.5), deps=[f_s5])
    f_sc = P.op(V, lambda v: v.tensor_tensor(
        out=AP(a_ck[:].tensor, 0, [[apck, 128], [256, CS], [1, 256]]),
        in0=AP(a_ck[:].tensor, 0, [[apck, 128], [256, CS], [1, 256]]),
        in1=AP(s_ck[:].tensor, 0, [[spck, 128], [1, CS], [0, 256]]),
        op=ALU.mult), deps=[fl_a, f_s6])

    st_Bd = ExitStack()
    wd_t = sbuf(st_Bd, "wd_t", [128, 32 * 1024], BF16)
    shq = sbuf(st_Bd, "shq", [128, INNER], BF16)
    hq_fm = sbuf(st_Bd, "hq_fm", [128, 32 * 128], BF16)
    yd_sb = sbuf(st_Bd, "yd_sb", [128, D], F32)
    xn2 = sbuf(st_Bd, "xn2", [128, D], BF16)
    xnd = yd_sb
    po = [sbuf(st_Bd, f"po{i}", [128, D], F32) for i in range(2)]
    c_dtile = sbuf(st_Bd, "c_dtile", [128, D], F32)
    p_d = [psum(st_Bd, f"p_d{i}", [128, 512], F32) for i in range(4)]

    f_dtile = P.dma(S, "bc1", c_dtile[:], drow.partition_broadcast(128),
                    deps=barrier_A)
    fl_wd = []
    for k in range(32):
        prev_same = [fl_wd[k - 4]] if k >= 4 else barrier_A
        fl_wd.append(P.dma(S if k % 2 == 0 else A, "wdl%d" % (k % 4),
                           wd_t[:, k * 1024:(k + 1) * 1024],
                           d_wdt[k * 128:(k + 1) * 128, :],
                           deps=prev_same))

    slotB_hq = [None]
    slotB_x = list(slot_x)
    slotB_po = [None, None]
    ptrB = dict(ptr_free)
    pdB = [None, None, None, None]
    prevB = {"t1": None, "t2": None, "xnd": None}
    bdown_last = [None]

    btail = [None]

    def emit_bdown(c):
        b = c % 2
        if btail[0] is not None:
            btail[0]()
            btail[0] = None
        fhl = P.dma(S, "hqb", shq[:], d_hq[c * 128:(c + 1) * 128, :],
                    deps=[a_fhq[c], slotB_hq[0]])
        fxl = P.dma(S, "xa%d" % b, sx[b][:], x_in[c * 128:(c + 1) * 128, :],
                    deps=[slotB_x[b]])
        ev4 = []
        ft = None
        for q in range(4):
            pt = p_tra if q % 2 == 0 else p_trb
            for kk in range(8):
                k = q * 8 + kk
                ft = P.op(T, lambda t, k=k, kk=kk, pt=pt: t.transpose(
                    pt[:, kk * 128:(kk + 1) * 128],
                    shq[:, k * 128:(k + 1) * 128], c_ident[:]),
                    deps=[fhl, f_ident] + ([ptrB[q % 2]] if kk == 0 else []))
            fe = P.op(A, lambda a, q=q, pt=pt: a.copy(
                hq_fm[:, q * 1024:(q + 1) * 1024], pt[:, 0:1024]), deps=[ft])
            ptrB[q % 2] = fe
            ev4.append(fe)
        slotB_hq[0] = ft
        pda = (c % 2) * 2
        mmd = [None, None]
        for k in range(32):
            for n in range(2):
                deps = []
                if k % 8 == 0 and n == 0:
                    deps.append(ev4[k // 8])
                if n == 0:
                    deps.append(fl_wd[k])
                if k == 0:
                    deps.append(pdB[pda + n])
                mmd[n] = P.op(T, lambda t, k=k, n=n, pda=pda: t.matmul(
                    p_d[pda + n][:, 0:512], hq_fm[:, k * 128:(k + 1) * 128],
                    wd_t[:, k * 1024 + n * 512: k * 1024 + n * 512 + 512],
                    start=(k == 0), stop=(k == 31)), deps=deps)
        fyd = []
        for n in range(2):
            f_ = P.op(A, lambda a, n=n, pda=pda, c=c: a.activation(
                out=yd_sb[:, n * 512:(n + 1) * 512], in_=p_d[pda + n][:, 0:512],
                func=AF.Identity, scale=tabs["cd"][:, c:c + 1]),
                deps=[mmd[n], prevB["t2"]])
            fyd.append(f_)
            pdB[pda + n] = f_
        fxn2 = P.op(A, lambda a, b=b, c=c: a.activation(
            out=xn2[:], in_=sx[b][:], func=AF.Identity,
            bias=tabs["beta"][:, c:c + 1], scale=tabs["rstd"][:, c:c + 1]),
            deps=[fxl, prevB["xnd"]])

        def emit_btail(c=c, b=b, fxl=fxl, fyd=fyd, fxn2=fxn2):
            ft1 = P.op(G, lambda g, b=b: g.tensor_tensor(
                out=po[b][:], in0=sx[b][:], in1=yd_sb[:], op=ALU.add),
                deps=[fxl, fyd[0], fyd[1], slotB_po[b]])
            slotB_x[b] = ft1
            fxnd = P.op(V, lambda v: v.tensor_tensor(
                out=xnd[:], in0=xn2[:], in1=c_dtile[:], op=ALU.mult),
                deps=[fxn2, f_dtile, ft1])
            ft2 = P.op(G, lambda g, b=b: g.tensor_tensor(
                out=po[b][:], in0=po[b][:], in1=xnd[:], op=ALU.add),
                deps=[ft1, fxnd])
            fpo = P.dma(S, "po%d" % b, d_part[c * 128:(c + 1) * 128, :], po[b][:],
                        deps=[ft2])
            slotB_po[b] = fpo
            prevB["t1"] = ft1
            prevB["xnd"] = fxnd
            prevB["t2"] = ft2
            bdown_last[0] = fpo
        btail[0] = emit_btail

    f_xc = P.op(V, lambda v: v.tensor_copy(xcar[:], c_xinit[:]),
                deps=[f_xinit] + barrier_A[0:1])
    fst_l = [None] * CS
    fw1_prev = None
    fba_l = [None] * CS
    fr1_l = [None] * (CS + 1)
    fw3_l = [None] * CS
    fp1_l = [None] * (CS + 1)

    def mk_fp1(s):
        base = s * 256
        return P.op(V, lambda v, base=base: v.tensor_tensor(
            out=AP(prod[:].tensor, 0, [[ppck, 128], [256, 16], [16, 16], [1, 16]]),
            in0=AP(a_ck[:].tensor, base, [[apck, 128], [16, 16], [0, 16], [1, 16]]),
            in1=AP(a_ck[:].tensor, base, [[apck, 128], [0, 16], [16, 16], [1, 16]]),
            op=ALU.mult),
            deps=[f_sc, fr1_l[s - 1] if s > 0 else None,
                  barrier_A[4] if s == 0 else None])

    def mk_pp(s):
        return P.op(G, lambda g, s=s: g.tensor_tensor(
            out=AP(pfx[:].tensor, 0, [[fpit, 128], [256, 17], [16, 16], [1, 16]]),
            in0=AP(xcar[:].tensor, 0, [[xpit, 128], [16, 17], [0, 16], [1, 16]]),
            in1=AP(w_ck[:].tensor, s * 256, [[wpck, 128], [0, 17], [16, 16], [1, 16]]),
            op=ALU.mult),
            deps=[fba_l[s - 1] if s > 0 else f_xc, fw3_l[s]])

    def mk_rr(s, fpp):
        frr = P.op(V, lambda v: v.tensor_reduce(
            out=AP(xcar[:].tensor, 0, [[xpit, 128], [16, 17], [1, 16]]),
            in_=AP(pfx[:].tensor, 0, [[fpit, 128], [16, 272], [1, 16]]),
            axis=AX.X, op=ALU.add),
            deps=[fpp, fst_l[s - 1] if s > 0 else None])
        fba = P.op(V, lambda v, s=s: v.tensor_tensor(
            out=xcar[:, 256:272], in0=xcar[:, 256:272],
            in1=b_ck[:, s * 16:(s + 1) * 16], op=ALU.add), deps=[frr, fl_b])
        fba_l[s] = fba

    def mk_st(s):
        fst_l[s] = P.dma(S, "xst",
                         AP(d_xsto.tensor, s * 272, [[CS * 272, 128], [1, 272]]),
                         xcar[:], deps=[fba_l[s]])

    fp1_l[0] = mk_fp1(0)
    fpp_p = None
    for s in range(CS):
        base = s * 256
        if s > 0:
            fpp_p = mk_pp(s - 1)
        fr1 = P.op(V, lambda v: v.tensor_reduce(
            out=bsq[:], in_=AP(prod[:].tensor, 0,
                               [[ppck, 128], [16, 256], [1, 16]]),
            axis=AX.X, op=ALU.add), deps=[fp1_l[s], fw1_prev])
        fr1_l[s] = fr1
        if s + 1 < CS:
            fp1_l[s + 1] = mk_fp1(s + 1)
        fe_ = P.op(V, lambda v, base=base: v.tensor_tensor(
            out=e_t[:], in0=bsq[:], in1=a_ck[:, base:base + 256], op=ALU.subtract),
            deps=[fr1])
        ff_ = P.op(V, lambda v, base=base: v.tensor_tensor(
            out=f_t[:], in0=bsq[:], in1=a_ck[:, base:base + 256], op=ALU.add),
            deps=[fe_])
        fp2 = P.op(V, lambda v: v.tensor_tensor(
            out=AP(prod[:].tensor, 4096, [[ppck, 128], [256, 16], [16, 16], [1, 16]]),
            in0=AP(f_t[:].tensor, 0, [[fpt, 128], [16, 16], [0, 16], [1, 16]]),
            in1=AP(e_t[:].tensor, 0, [[ept, 128], [0, 16], [16, 16], [1, 16]]),
            op=ALU.mult), deps=[fe_, ff_])
        fr2 = P.op(V, lambda v, s=s: v.tensor_reduce(
            out=w_ck[:, s * 256:(s + 1) * 256],
            in_=AP(prod[:].tensor, 4096, [[ppck, 128], [16, 256], [1, 16]]),
            axis=AX.X, op=ALU.add), deps=[fp2])
        if s > 0:
            mk_rr(s - 1, fpp_p)
        fw1a = P.op(V, lambda v, base=base: v.tensor_scalar(
            out=e_t[:], in0=a_ck[:, base:base + 256], scalar1=-2.0,
            scalar2=None, op0=ALU.mult), deps=[fr2, fp2])
        fw1 = P.op(V, lambda v: v.tensor_tensor(
            out=e_t[:], in0=e_t[:], in1=bsq[:], op=ALU.subtract), deps=[fw1a])
        fw1_prev = fw1
        if s > 0:
            mk_st(s - 1)
        fw2 = P.op(V, lambda v, s=s: v.tensor_tensor(
            out=w_ck[:, s * 256:(s + 1) * 256], in0=w_ck[:, s * 256:(s + 1) * 256],
            in1=e_t[:], op=ALU.add), deps=[fw1])
        fw3 = P.op(V, lambda v, s=s: v.tensor_tensor(
            out=w_ck[:, s * 256:(s + 1) * 256], in0=w_ck[:, s * 256:(s + 1) * 256],
            in1=c_itile[:], op=ALU.add), deps=[fw2, f_itile])
        fw3_l[s] = fw3
        emit_bdown(s)
    fpp_p = mk_pp(CS - 1)
    mk_rr(CS - 1, fpp_p)
    mk_st(CS - 1)
    if btail[0] is not None:
        btail[0]()
        btail[0] = None
    fstep = fst_l[CS - 1]

    f_xcd = P.dma(S, "msc", d_xc, xcar[:], deps=[fstep])
    f_h00 = P.op(V, lambda v: v.memset(combh[:], 0.0), deps=[f_xcd])
    f_h0s = P.op(V, lambda v: v.memset(h0sq[:], 0.0), deps=[f_h00])
    fprevh = f_h0s
    f_cmb = None
    for j in range(64):
        if j % 8 == 0:
            f_cmb = P.dma(S, "msc", comb[:],
                          AP(d_xc.tensor, j * 272,
                             [[64 * 272, 2], [1, 8 * 272]]),
                          deps=[f_xcd, fprevh])
        jj = j % 8
        cur = (j % 2) * 16
        nxt = ((j + 1) % 2) * 16
        fmv = P.op(V, lambda v, jj=jj, cur=cur: v.tensor_tensor(
            out=AP(combs[:].tensor, 0, [[combs[:].ap[0][0], 2], [1, 16], [16, 16]]),
            in0=AP(comb[:].tensor, jj * 272, [[cpit, 2], [1, 16], [16, 16]]),
            in1=AP(combh[:].tensor, cur, [[hpit, 2], [0, 16], [1, 16]]),
            op=ALU.mult), deps=[fprevh, f_cmb])
        frv = P.op(V, lambda v: v.tensor_reduce(
            out=AP(combh[:].tensor, 32, [[hpit, 2], [1, 16]]),
            in_=AP(combs[:].tensor, 0, [[combs[:].ap[0][0], 2], [1, 16], [16, 16]]),
            axis=AX.X, op=ALU.add), deps=[fmv])
        fad2 = P.op(V, lambda v, jj=jj, nxt=nxt: v.tensor_tensor(
            out=AP(combh[:].tensor, nxt, [[hpit, 2], [1, 16]]),
            in0=AP(combh[:].tensor, 32, [[hpit, 2], [1, 16]]),
            in1=AP(comb[:].tensor, jj * 272 + 256, [[cpit, 2], [1, 16]]),
            op=ALU.add), deps=[frv])
        if j < 63:
            fprevh = P.op(V, lambda v, j=j, nxt=nxt: v.tensor_copy(
                h0sq[:, (j + 1) * 16:(j + 2) * 16],
                AP(combh[:].tensor, nxt, [[hpit, 2], [1, 16]])), deps=[fad2])
        else:
            fprevh = fad2
    f_h0d = P.dma(S, "msc", AP(d_h0.tensor, 0, [[64 * 16, 2], [1, 64 * 16]]),
                  h0sq[:], deps=[fprevh])
    f_h0l = P.dma(S, "msc", h0ck[:], d_h0, deps=[f_h0d])

    st_Bd.close()

    st_C = ExitStack()
    xsto_f = sbuf(st_C, "xsto_f", [128, CS * 272], F32)
    h_tmc = sbuf(st_C, "h_tmc", [128, 16], BF16)
    h_fmT = sbuf(st_C, "h_fmT", [16, 128], BF16)
    pin = [sbuf(st_C, f"pin{i}", [128, D], F32) for i in range(2)]
    so = [sbuf(st_C, f"so{i}", [128, D], F32) for i in range(2)]
    c_cw = sbuf(st_C, "c_cw", [16, D], BF16)
    p_y = psum(st_C, "p_y", [128, 1024], F32)
    xspit = xsto_f[:].ap[0][0]

    f_cw = P.dma(G, "bc2", c_cw[:], cwT_in, deps=barrier_A)
    fxsl = P.dma(S, "xsl", xsto_f[:], d_xsto,
                 deps=[fst_l[CS - 1], bdown_last[0], slotB_po[0], slotB_po[1],
                       pdB[0], pdB[1], pdB[2], pdB[3], ptrB[0], ptrB[1],
                       slotB_hq[0], prevB["xnd"], prevB["t2"]])
    fre1 = P.op(G, lambda g: g.tensor_tensor(
        out=AP(prod[:].tensor, 0, [[ppck, 128], [256, 32], [16, 16], [1, 16]]),
        in0=AP(xsto_f[:].tensor, 0, [[xspit, 128], [272, 32], [1, 16], [16, 16]]),
        in1=AP(h0ck[:].tensor, 0, [[h0pit, 128], [0, 32], [0, 16], [1, 16]]),
        op=ALU.mult), deps=[f_h0l, fxsl])
    fre2 = P.op(V, lambda v: v.tensor_reduce(
        out=hall[:], in_=AP(prod[:].tensor, 0,
                            [[ppck, 128], [256, 32], [16, 16], [1, 16]]),
        axis=AX.X, op=ALU.add), deps=[fre1])
    fre3 = P.op(V, lambda v: v.tensor_tensor(
        out=hall[:], in0=hall[:],
        in1=AP(xsto_f[:].tensor, 256, [[xspit, 128], [272, 32], [1, 16]]),
        op=ALU.add), deps=[fre2])
    f_htm = P.dma(G, "htms", AP(d_htm.tensor, 0, [[CS * 16, 128], [1, CS * 16]]),
                  hall[:], deps=[fre3])

    slotC_pin = [None, None]
    slotC_out = [None, None]
    prev_fth = None
    feh_p = ptrB[1]
    pyC = None
    for c in range(NCH):
        b = c % 2
        fh2 = P.dma(S, "htm2", h_tmc[:],
                    AP(d_htm.tensor, c * 2048, [[16, 128], [1, 16]]),
                    deps=[f_htm if c == 0 else prev_fth])
        fth = P.op(T, lambda t: t.transpose(p_trb[0:16, 0:128], h_tmc[:],
                                            c_ident[:]), deps=[fh2, feh_p])
        prev_fth = fth
        feh = P.op(A, lambda a: a.copy(h_fmT[:], p_trb[0:16, 0:128]), deps=[fth])
        feh_p = feh
        fy1 = P.op(T, lambda t: t.matmul(p_y[:, 0:512], h_fmT[:], c_cw[:, 0:512],
                                         start=True, stop=True),
                   deps=[feh, f_cw, pyC])
        fy2 = P.op(T, lambda t: t.matmul(p_y[:, 512:1024], h_fmT[:],
                                         c_cw[:, 512:1024], start=True, stop=True),
                   deps=[fy1])
        fpl = P.dma(A, "pi%d" % b, pin[b][:], d_part[c * 128:(c + 1) * 128, :],
                    deps=[slotB_po[b] if c < 2 else None, slotC_pin[b]])
        fo = P.op(V, lambda v, b=b: v.tensor_tensor(
            out=so[b][:], in0=pin[b][:], in1=p_y[:, 0:1024], op=ALU.add),
            deps=[fy2, fpl, slotC_out[b]])
        pyC = fo
        slotC_pin[b] = fo
        fod = P.dma(S, "out%d" % b, y_out[c * 128:(c + 1) * 128, :], so[b][:],
                    deps=[fo])
        slotC_out[b] = fod

    st_C.close()
    st_M.close()
    fin = P.barrier()
    P.op(S, lambda e: e.nop(), deps=fin, inc=0)

    st_sem = ExitStack()
    sems = {e: st_sem.enter_context(nc.semaphore("sem_" + e))
            for e in list(ENGINES) + sorted(P.streams)}
    with nc.Block() as block:
        with nc.allow_low_precision(reason="quantized ternary/int8 data is exact in bf16"):
            P.emit(block, sems)
    st_sem.close()
    st_top.close()
    return nc


_CACHE = {}


def _rconst():
    if "r" not in _CACHE:
        import jax
        with jax.default_device(jax.devices("cpu")[0]):
            key = jax.random.key(1, impl="threefry2x32")
            r = np.asarray(jax.random.normal(key, (16, 2048, 16, 1),
                                             jax.numpy.float32))[..., 0]
        jax.config.update("jax_enable_x64", True)
        with jax.default_device(jax.devices("cpu")[0]):
            key = jax.random.key(1, impl="threefry2x32")
            r64 = np.asarray(jax.random.normal(key, (16, 2048, 16, 1),
                                               jax.numpy.float64))[..., 0]
        jax.config.update("jax_enable_x64", False)
        _CACHE["r"] = r
        _CACHE["r64"] = r64
    return _CACHE["r"]


def make_in_maps(L0_attn_u, L0_attn_W_proj_w, L0_attn_W_proj_b, L0_attn_B_w,
                 L0_attn_C_w, L0_attn_D, L0_ffn_w_gate, L0_ffn_w_up, L0_ffn_w_down):
    r = _rconst()
    f32 = np.float32
    wgT = np.ascontiguousarray(np.asarray(L0_ffn_w_gate, f32).T)
    wuT = np.ascontiguousarray(np.asarray(L0_ffn_w_up, f32).T)
    wdT = np.ascontiguousarray(np.asarray(L0_ffn_w_down, f32).T)
    cwT = np.ascontiguousarray(np.asarray(L0_attn_C_w, f32).T)
    Wp = np.asarray(L0_attn_W_proj_w, f32)
    Wpb = np.asarray(L0_attn_W_proj_b, f32)
    Bw = np.asarray(L0_attn_B_w, f32)
    Dv = np.asarray(L0_attn_D, f32)
    x = np.asarray(L0_attn_u, f32)

    rows, cols = np.triu_indices(16, 1)
    pidx = {(int(a), int(bb)): k for k, (a, bb) in enumerate(zip(rows, cols))}
    pw_cols = np.zeros((256, 1024), f32)
    pb_row = np.zeros((1, 256), f32)
    for d in range(1, 16):
        for i in range(16 - d):
            k = pidx[(i, i + d)]
            pw_cols[(d - 1) * 16 + i] = Wp[k]
            pb_row[0, (d - 1) * 16 + i] = Wpb[k]
    pw_cols[240:256] = Bw
    pw_host = np.ascontiguousarray(pw_cols.T)

    ident = np.eye(128, dtype=BF)
    ones1b = np.ones((1, 128), dtype=BF)
    ones1f = np.ones((1, 128), dtype=f32)
    onescol = np.ones((1, 1), dtype=f32)
    irow = np.zeros((1, 256), dtype=BF)
    for i in range(16):
        irow[0, i * 17] = BF(1.0)
    xinit = np.zeros((1, 272), f32)
    for i in range(16):
        xinit[0, i * 16 + i] = 1.0
    drow_host = np.ascontiguousarray(Dv.reshape(1, D))

    r64 = _CACHE["r64"]

    def _arrange(rr):
        return np.ascontiguousarray(
            rr.reshape(NCH, 128, 16).transpose(1, 0, 2).reshape(128, NCH * 16))

    in_maps = []
    for core in range(NCORES):
        xs = np.ascontiguousarray(x[2 * core:2 * core + 2].reshape(TOK, D))
        rc = _arrange(r[2 * core:2 * core + 2].reshape(TOK, 16).astype(f32))
        r64s = r64[2 * core:2 * core + 2].reshape(TOK, 16)
        rbh = r64s.astype(f32)
        rbl = (r64s - rbh.astype(np.float64)).astype(f32)
        in_maps.append(dict(
            x=xs, wgT=wgT, wuT=wuT, wdT=wdT, pw=pw_host, pbrow=pb_row,
            cwT=cwT, drow=drow_host, rc=rc, rbh=_arrange(rbh), rbl=_arrange(rbl),
            ident=ident, ones1b=ones1b,
            ones1f=ones1f, onescol=onescol, irow=irow, xinit=xinit,
        ))
    return in_maps


def kernel(L0_attn_u, L0_attn_W_proj_w, L0_attn_W_proj_b, L0_attn_B_w, L0_attn_C_w,
           L0_attn_D, L0_ffn_w_gate, L0_ffn_w_up, L0_ffn_w_down):
    in_maps = make_in_maps(L0_attn_u, L0_attn_W_proj_w, L0_attn_W_proj_b,
                           L0_attn_B_w, L0_attn_C_w, L0_attn_D, L0_ffn_w_gate,
                           L0_ffn_w_up, L0_ffn_w_down)
    nc = _CACHE.get("nc")
    if nc is None:
        nc = build_program()
        _CACHE["nc"] = nc
    res = run_bass_kernel_spmd(nc, in_maps, core_ids=list(range(NCORES)))
    out = np.empty((16, 2048, D), np.float32)
    for core in range(NCORES):
        out[2 * core:2 * core + 2] = res.results[core]["y_out"].reshape(2, 2048, D)
    return out


# version 20
# speedup vs baseline: 1.0708x; 1.0708x over previous
import numpy as np
import ml_dtypes
import concourse.bass as bass
import concourse.mybir as mybir
from concourse.ap import AP
from concourse.bass_utils import run_bass_kernel_spmd

F32 = mybir.dt.float32
BF16 = mybir.dt.bfloat16
I32 = mybir.dt.int32
ALU = mybir.AluOpType
AF = mybir.ActivationFunctionType
AX = mybir.AxisListType
BF = ml_dtypes.bfloat16

NCORES = 8
TOK = 4096
D = 1024
INNER = 4096
NCH = 32
CS = 32
NSC = 128

ENGINES = ("sync", "vector", "scalar", "tensor", "gpsimd")
SERIAL_ENGINES = ("vector", "scalar", "gpsimd")


class Prog:

    def __init__(self, nc):
        self.nc = nc
        self.items = {e: [] for e in ENGINES}
        self.count = {}
        self.ecount = {e: 0 for e in ENGINES}
        self.streams = set()

    def op(self, eng, fn, deps=(), inc=1, sem=None):
        semname = sem or eng
        if sem is not None:
            self.streams.add(sem)
        waits = [d for d in deps if d is not None]
        self.items[eng].append((fn, waits, inc, semname))
        self.count[semname] = self.count.get(semname, 0) + inc
        if semname == eng:
            self.ecount[eng] += inc
        return (semname, self.count[semname])

    def dma(self, eng, stream, out, in_, deps=(), **kw):
        def fn(e, out=out, in_=in_, kw=kw):
            return e.dma_start(out=out, in_=in_, **kw)
        return self.op(eng, fn, deps, inc=16, sem=stream)

    def barrier(self):
        return [(sn, v) for sn, v in self.count.items() if v > 0]

    def emit(self, block, sems):
        for ename in ENGINES:
            if not self.items[ename]:
                continue

            def mk(ename, items):
                def body(eng):
                    water = {}
                    myec = 0
                    serial = ename in SERIAL_ENGINES
                    for fn, waits, inc, semname in items:
                        if serial and myec > water.get(ename, 0):
                            eng.wait_ge(sems[ename], myec)
                            water[ename] = myec
                        for (de, dv) in waits:
                            if de == ename and dv > myec:
                                raise RuntimeError("forward same-engine dep")
                            if dv > water.get(de, 0):
                                eng.wait_ge(sems[de], dv)
                                water[de] = dv
                        inst = fn(eng)
                        if inc > 0:
                            inst.then_inc(sems[semname], inc)
                        if semname == ename:
                            myec += inc
                return body

            getattr(block, ename)(mk(ename, self.items[ename]))


def apx(t, offset, dims):
    full = t[:]
    pitch = full.ap[0][0]
    nparts = full.ap[0][1]
    return AP(full.tensor, offset, [[pitch, nparts]] + [list(d) for d in dims])


def build_program():
    nc = bass.Bass("TRN2", target_bir_lowering=False)

    x_in = nc.dram_tensor("x", [TOK, D], F32, kind="ExternalInput").ap()
    wgT = nc.dram_tensor("wgT", [D, INNER], F32, kind="ExternalInput").ap()
    wuT = nc.dram_tensor("wuT", [D, INNER], F32, kind="ExternalInput").ap()
    wdT = nc.dram_tensor("wdT", [INNER, D], F32, kind="ExternalInput").ap()
    pw = nc.dram_tensor("pw", [D, 256], F32, kind="ExternalInput").ap()
    pbrow = nc.dram_tensor("pbrow", [1, 256], F32, kind="ExternalInput").ap()
    cwT_in = nc.dram_tensor("cwT", [16, D], F32, kind="ExternalInput").ap()
    drow = nc.dram_tensor("drow", [1, D], F32, kind="ExternalInput").ap()
    rc_in = nc.dram_tensor("rc", [128, NCH * 16], F32, kind="ExternalInput").ap()
    rbh_in = nc.dram_tensor("rbh", [128, NCH * 16], F32, kind="ExternalInput").ap()
    rbl_in = nc.dram_tensor("rbl", [128, NCH * 16], F32, kind="ExternalInput").ap()
    ident_in = nc.dram_tensor("ident", [128, 128], BF16, kind="ExternalInput").ap()
    ones1b_in = nc.dram_tensor("ones1b", [1, 128], BF16, kind="ExternalInput").ap()
    ones1f_in = nc.dram_tensor("ones1f", [1, 128], F32, kind="ExternalInput").ap()
    onescol_in = nc.dram_tensor("onescol", [1, 1], F32, kind="ExternalInput").ap()
    irow_in = nc.dram_tensor("irow", [1, 256], BF16, kind="ExternalInput").ap()
    xinit_in = nc.dram_tensor("xinit", [1, 272], F32, kind="ExternalInput").ap()
    y_out = nc.dram_tensor("y_out", [TOK, D], F32, kind="ExternalOutput").ap()

    d_abar = nc.dram_tensor("d_abar", [TOK, 256], BF16, kind="Internal").ap()
    d_bx = nc.dram_tensor("d_bx", [TOK, 16], F32, kind="Internal").ap()
    d_ss = nc.dram_tensor("d_ss", [TOK, 2], F32, kind="Internal").ap()
    d_hq = nc.dram_tensor("d_hq", [TOK, INNER], BF16, kind="Internal").ap()
    d_wdt = nc.dram_tensor("d_wdt", [INNER, D], BF16, kind="Internal").ap()
    d_htm = nc.dram_tensor("d_htm", [TOK, 16], BF16, kind="Internal").ap()
    d_h0 = nc.dram_tensor("d_h0", [NSC, 16], F32, kind="Internal").ap()
    d_xc = nc.dram_tensor("d_xc", [NSC, 272], F32, kind="Internal").ap()
    d_xsto = nc.dram_tensor("d_xsto", [128, 32 * 272], F32, kind="Internal").ap()
    d_part = nc.dram_tensor("d_part", [TOK, D], F32, kind="Internal").ap()


    P = Prog(nc)
    S, V, A, T, G = "sync", "vector", "scalar", "tensor", "gpsimd"

    from contextlib import ExitStack
    st_top = ExitStack()
    st_A = ExitStack()
    st_pro = ExitStack()
    st_A2 = ExitStack()

    def sbuf(st, name, shape, dt):
        return st.enter_context(nc.sbuf_tensor(name, shape, dt))

    def psum(st, name, shape, dt):
        return st.enter_context(nc.psum_tensor(name, shape, dt))

    c_ident = sbuf(st_top, "c_ident", [128, 128], BF16)
    c_ones1b = sbuf(st_top, "c_ones1b", [1, 128], BF16)
    c_ones1f = sbuf(st_top, "c_ones1f", [1, 128], F32)
    c_onescol = sbuf(st_top, "c_onescol", [128, 1], F32)
    c_rc = sbuf(st_top, "c_rc", [128, NCH * 16], F32)
    c_itile = sbuf(st_top, "c_itile", [128, 256], BF16)
    c_xinit = sbuf(st_top, "c_xinit", [128, 272], F32)
    c_pw = sbuf(st_top, "c_pw", [128, 8 * 256], BF16)
    c_pbrow = sbuf(st_top, "c_pbrow", [1, 256], BF16)
    tabs = {}
    for nm in ["rstd", "beta", "alpha", "betaq", "cg", "cu", "cd"]:
        tabs[nm] = sbuf(st_top, "tab_" + nm, [128, NCH], F32)
    sx = [sbuf(st_top, f"sx{i}", [128, D], F32) for i in range(2)]
    qy = sbuf(st_top, "qy", [128, D], F32)
    qi = sbuf(st_top, "qi", [128, D], I32)
    qm = sbuf(st_top, "qm", [128, D], F32)
    sw = {w: sbuf(st_top, f"sw_{w}", [128, 1], F32) for w in ["g", "u", "d"]}
    t1c = sbuf(st_top, "t1c", [128, 1], F32)
    t2c = sbuf(st_top, "t2c", [128, 1], F32)
    t3c = sbuf(st_top, "t3c", [128, 1], F32)
    swg_r = sbuf(st_top, "swg_r", [128, 1], F32)
    swu_r = sbuf(st_top, "swu_r", [128, 1], F32)
    bnst = sbuf(st_top, "bnst", [128, 12], F32)
    bnag = sbuf(st_top, "bnag", [128, 2], F32)

    p_tra = psum(st_top, "p_tra", [128, 1024], BF16)
    p_trb = psum(st_top, "p_trb", [128, 1024], BF16)

    c_rb = sbuf(st_top, "c_rb", [128, NCH * 16], F32)
    c_msk = sbuf(st_top, "c_msk", [128, 1], F32)

    c_half = sbuf(st_top, "c_half", [128, 1], F32)
    c_eps = sbuf(st_top, "c_eps", [128, 1], F32)
    P.op(V, lambda v: v.memset(c_half[:], 0.5))
    P.op(V, lambda v: v.memset(c_eps[:], 1e-5))
    nc.const_aps.register(F32, 0.5, c_half[:, 0:1])
    nc.const_aps.register(F32, 1e-5, c_eps[:, 0:1])

    P.dma(S, "cst", c_ident[:], ident_in)
    P.dma(S, "cst", c_ones1b[:], ones1b_in)
    P.dma(S, "cst", c_ones1f[:], ones1f_in)
    P.dma(S, "cst", c_onescol[:], onescol_in.partition_broadcast(128))
    P.dma(S, "cst", c_rc[:], rc_in)
    P.dma(S, "cst", c_rb[:], rbh_in)
    P.dma(S, "cst", qy[:, 0:NCH * 16], rbl_in)
    P.dma(S, "cst", c_itile[:], irow_in.partition_broadcast(128))
    f_cst = P.dma(S, "cst", c_xinit[:], xinit_in.partition_broadcast(128))
    f_ident = f_ones1b = f_ones1f = f_onescol = f_itile = f_xinit = f_cst
    fm1 = P.op(V, lambda v: v.memset(c_msk[:], 1.0), deps=[f_cst])
    fm2 = P.op(V, lambda v: v.tensor_scalar(
        out=c_msk[:], in0=c_msk[:], scalar1=2.0 ** -30, scalar2=-1.0,
        op0=ALU.add, op1=ALU.add), deps=[fm1])
    fm3 = P.op(V, lambda v: v.tensor_scalar(
        out=c_msk[:], in0=c_msk[:], scalar1=0.0, scalar2=None, op0=ALU.is_gt),
        deps=[fm2])
    fm4 = P.op(V, lambda v: v.tensor_tensor(
        out=c_rb[:], in0=c_rb[:], in1=qy[:, 0:NCH * 16], op=ALU.add), deps=[fm3])
    fm5 = P.op(V, lambda v: v.scalar_tensor_tensor(
        out=c_rb[:], in0=c_rb[:], scalar=c_msk[:], in1=c_rc[:],
        op0=ALU.mult, op1=ALU.subtract), deps=[fm4])
    f_rc = P.op(V, lambda v: v.scalar_tensor_tensor(
        out=c_rc[:], in0=c_rb[:], scalar=c_msk[:], in1=c_rc[:],
        op0=ALU.mult, op1=ALU.add), deps=[fm5])
    pw_src = AP(pw.tensor, 0, [[256, 128], [128 * 256, 8], [1, 256]])
    P.dma(G, "gcst", apx(c_pw, 0, [[256, 8], [1, 256]]), pw_src)
    f_gcst = P.dma(G, "gcst", c_pbrow[:], pbrow)
    f_pw = f_pb = f_gcst

    wg_t = sbuf(st_A, "wg_t", [128, 8 * 4096], BF16)
    wu_t = sbuf(st_A, "wu_t", [128, 8 * 4096], BF16)
    wtmp = [sbuf(st_pro, f"wtmp{i}", [128, 4096], F32) for i in range(2)]
    wabs = sbuf(st_pro, "wabs", [128, 32], F32)
    wsum1 = sbuf(st_pro, "wsum1", [1, 1], F32)
    wd4 = sbuf(st_pro, "wd4", [128, 1024], BF16)
    p_sum = psum(st_pro, "p_sum", [128, 512], F32)

    tstate = {"buf_free": [None, None], "prev_ea": None, "prev_ua": None,
              "prev_dma": None, "pre_c": 0, "prev_f": None}

    def tern_pass1(src_ap, n_kt, kw, sw_col, extra_dep=None):
        reds = []
        bf = tstate["buf_free"]
        for k in range(n_kt):
            wb = wtmp[k % 2]
            fl = P.dma(S if k % 2 == 0 else G, "wt%d" % (k % 2), wb[:, 0:kw],
                       src_ap[k * 128:(k + 1) * 128, :], deps=[bf[k % 2]])
            fr = P.op(V, lambda v, wb=wb, k=k, kw=kw: v.tensor_reduce(
                out=wabs[:, k:k + 1], in_=wb[:, 0:kw],
                axis=AX.X, op=ALU.add, apply_absolute_value=True), deps=[fl])
            bf[k % 2] = fr
            reds.append(fr)
        fs8 = P.op(V, lambda v, n_kt=n_kt: v.tensor_reduce(
            out=wabs[:, 0:1], in_=wabs[:, 0:n_kt], axis=AX.X, op=ALU.add),
            deps=reds)
        fmm = P.op(T, lambda t: t.matmul(p_sum[0:1, 0:1], wabs[:, 0:1],
                                         c_onescol[:], start=True, stop=True),
                   deps=[fs8, f_onescol, extra_dep])
        fev = P.op(V, lambda v: v.tensor_copy(wsum1[:], p_sum[0:1, 0:1]),
                   deps=[fmm])
        fbc = P.op(T, lambda t: t.matmul(p_sum[:, 0:1],
                                         apx(c_ones1f, 0, [[1, 128]]), wsum1[:],
                                         start=True, stop=True),
                   deps=[fev, f_ones1f])
        n_el = n_kt * 128 * kw
        f1 = P.op(V, lambda v, n_el=n_el: v.tensor_scalar(
            out=sw_col[:], in0=p_sum[:, 0:1], scalar1=1.0 / n_el, scalar2=1e-5,
            op0=ALU.mult, op1=ALU.max), deps=[fbc])
        return P.op(V, lambda v: v.reciprocal(out=sw_col[:], in_=sw_col[:]),
                    deps=[f1])

    def emit_prepass(c):
        b = c % 2
        fx = P.dma(A, "xt%d" % b, sx[b][:], x_in[c * 128:(c + 1) * 128, :],
                   deps=[pre_done[c - 2] if c >= 2 else None])
        f1 = P.op(V, lambda v, b=b: v.bn_stats(bnst[:, 0:6], sx[b][:, 0:512]),
                  deps=[fx, tstate["prev_f"]])
        f2 = P.op(V, lambda v, b=b: v.bn_stats(bnst[:, 6:12], sx[b][:, 512:1024]),
                  deps=[f1])
        f3 = P.op(V, lambda v: v.bn_aggr(bnag[:], bnst[:]), deps=[f2])
        f4 = P.op(A, lambda s, c=c: s.activation(
            out=tabs["rstd"][:, c:c + 1], in_=bnag[:, 1:2], func=AF.Ln, bias=1e-5),
            deps=[f3])
        f5 = P.op(A, lambda s, c=c: s.activation(
            out=tabs["rstd"][:, c:c + 1], in_=tabs["rstd"][:, c:c + 1], func=AF.Exp,
            scale=-0.5), deps=[f4])
        fmx = P.op(V, lambda v, b=b: v.tensor_reduce(
            out=t1c[:], in_=sx[b][:], axis=AX.X, op=ALU.max), deps=[fx, f3])
        fmn = P.op(V, lambda v, b=b: v.tensor_reduce(
            out=t3c[:], in_=sx[b][:], axis=AX.X, op=ALU.min), deps=[fmx])
        fz1 = P.op(V, lambda v: v.tensor_scalar(
            out=t1c[:], in0=t1c[:], scalar1=bnag[:, 0:1], scalar2=None,
            op0=ALU.subtract), deps=[fmx, f3])
        fz2 = P.op(V, lambda v: v.scalar_tensor_tensor(
            out=t3c[:], in0=t3c[:], scalar=-1.0, in1=bnag[:, 0:1],
            op0=ALU.mult, op1=ALU.add), deps=[fmn])
        f7 = P.op(V, lambda v: v.tensor_tensor(
            out=t1c[:], in0=t1c[:], in1=t3c[:], op=ALU.max), deps=[fz1, fz2])
        f8 = P.op(V, lambda v, c=c: v.tensor_tensor(
            out=t1c[:], in0=t1c[:], in1=tabs["rstd"][:, c:c + 1], op=ALU.mult),
            deps=[f7, f5])
        f8b = P.op(V, lambda v: v.tensor_scalar(
            out=t1c[:], in0=t1c[:], scalar1=1e-5, scalar2=None, op0=ALU.max),
            deps=[f8])
        f9 = P.op(V, lambda v: v.reciprocal(out=t2c[:], in_=t1c[:]), deps=[f8b])
        f10 = P.op(V, lambda v, c=c: v.scalar_tensor_tensor(
            out=tabs["alpha"][:, c:c + 1], in0=t2c[:], scalar=127.0,
            in1=tabs["rstd"][:, c:c + 1], op0=ALU.mult, op1=ALU.mult), deps=[f9])
        f11 = P.op(V, lambda v, c=c: v.scalar_tensor_tensor(
            out=tabs["beta"][:, c:c + 1], in0=bnag[:, 0:1], scalar=-1.0,
            in1=tabs["rstd"][:, c:c + 1], op0=ALU.mult, op1=ALU.mult), deps=[f3])
        f12 = P.op(V, lambda v, c=c: v.scalar_tensor_tensor(
            out=tabs["betaq"][:, c:c + 1], in0=bnag[:, 0:1], scalar=-1.0,
            in1=tabs["alpha"][:, c:c + 1], op0=ALU.mult, op1=ALU.mult), deps=[f10])
        f12b = P.op(V, lambda v, c=c: v.tensor_scalar(
            out=tabs["betaq"][:, c:c + 1], in0=tabs["betaq"][:, c:c + 1],
            scalar1=0.5, scalar2=None, op0=ALU.add), deps=[f12])
        f14 = P.op(V, lambda v, c=c: v.scalar_tensor_tensor(
            out=tabs["cg"][:, c:c + 1], in0=swg_r[:], scalar=1.0 / 127.0,
            in1=t1c[:], op0=ALU.mult, op1=ALU.mult), deps=[f_rswg, f12b])
        f16 = P.op(V, lambda v, c=c: v.scalar_tensor_tensor(
            out=tabs["cu"][:, c:c + 1], in0=swu_r[:], scalar=1.0 / 127.0,
            in1=t1c[:], op0=ALU.mult, op1=ALU.mult), deps=[f_rswu])
        tstate["prev_f"] = f16
        pre_done.append(f16)

    def tern_pass2(src_ap, n_kt, kw, dst_fn, sw_col, sw_dep, dram_out=None,
                   hook=None):
        tile_done = []
        bf = tstate["buf_free"]
        for k in range(n_kt):
            wb = wtmp[k % 2]
            fl = P.dma(S if k % 2 == 0 else G, "wt%d" % (k % 2), wb[:, 0:kw],
                       src_ap[k * 128:(k + 1) * 128, :], deps=[bf[k % 2]])
            subs = []
            wsub = 1024
            for s4 in range(kw // wsub):
                wsl = wb[:, s4 * wsub:(s4 + 1) * wsub]
                ya = P.op(A, lambda s, w=wsl: s.activation(
                    out=w, in_=w, func=AF.Identity, bias=0.5, scale=sw_col[:]),
                    deps=[fl, sw_dep])
                ia = P.op(V, lambda v, w=wsl, n=wsub: v.tensor_copy(
                    qi[:, 0:n], w), deps=[ya, tstate["prev_ua"]])
                ma = P.op(V, lambda v, w=wsl, n=wsub: v.tensor_tensor(
                    out=qm[:, 0:n], in0=w, in1=qi[:, 0:n], op=ALU.subtract),
                    deps=[ia, tstate["prev_ea"]])
                ua = P.op(V, lambda v, n=wsub: v.scalar_tensor_tensor(
                    out=qy[:, 0:n], in0=qm[:, 0:n], scalar=1.0,
                    in1=qi[:, 0:n], op0=ALU.is_ge, op1=ALU.add), deps=[ma])
                tstate["prev_ua"] = ua
                ua = P.op(V, lambda v, n=wsub: v.scalar_tensor_tensor(
                    out=qy[:, 0:n], in0=qm[:, 0:n], scalar=-1.0,
                    in1=qy[:, 0:n], op0=ALU.is_lt, op1=ALU.subtract),
                    deps=[ua])
                ea = P.op(V, lambda v, w=wsl, n=wsub: v.scalar_tensor_tensor(
                    out=w, in0=qm[:, 0:n], scalar=0.0, in1=qy[:, 0:n],
                    op0=ALU.is_lt, op1=ALU.add), deps=[ua])
                tstate["prev_ea"] = ea
                if dram_out is None:
                    ca = P.op(V, lambda v, k=k, s4=s4, w=wsl, n=wsub: v.tensor_scalar(
                        out=dst_fn(k)[:, s4 * n:(s4 + 1) * n], in0=w,
                        scalar1=-1.0, scalar2=1.0, op0=ALU.max, op1=ALU.min),
                        deps=[ea])
                else:
                    ca = P.op(V, lambda v, w=wsl: v.tensor_scalar(
                        out=wd4[:], in0=w, scalar1=-1.0, scalar2=1.0,
                        op0=ALU.max, op1=ALU.min), deps=[ea, tstate["prev_dma"]])
                    tstate["prev_dma"] = P.dma(
                        S, "wdo", dram_out[k * 128:(k + 1) * 128, :],
                        wd4[:], deps=[ca])
                    ca = tstate["prev_dma"]
                subs.append(ca)
            bf[k % 2] = subs[-1]
            tile_done.append(subs[-1])
            if hook is not None:
                for _ in range(2):
                    if tstate["pre_c"] < NCH:
                        emit_prepass(tstate["pre_c"])
                        tstate["pre_c"] += 1
        return tile_done

    pre_done = []
    f_swg = tern_pass1(wgT, 8, 4096, sw["g"])
    f_swu = tern_pass1(wuT, 8, 4096, sw["u"], extra_dep=f_swg)
    f_rswg = P.op(V, lambda v: v.reciprocal(out=swg_r[:], in_=sw["g"][:]),
                  deps=[f_swg])
    f_rswu = P.op(V, lambda v: v.reciprocal(out=swu_r[:], in_=sw["u"][:]),
                  deps=[f_swu])
    fin_g = tern_pass2(wgT, 8, 4096,
                       lambda k: wg_t[:, k * 4096:(k + 1) * 4096], sw["g"],
                       f_swg, hook=True)
    fin_u = tern_pass2(wuT, 8, 4096,
                       lambda k: wu_t[:, k * 4096:(k + 1) * 4096], sw["u"],
                       f_swu, hook=True)
    f_swd = tern_pass1(wdT, 32, 1024, sw["d"], extra_dep=f_swu)
    fin_d = tern_pass2(wdT, 32, 1024, None, sw["d"], f_swd, dram_out=d_wdt)
    while tstate["pre_c"] < NCH:
        emit_prepass(tstate["pre_c"])
        tstate["pre_c"] += 1

    st_pro.close()
    prologue_done = P.barrier()

    qhy = sbuf(st_A2, "qhy", [128, 1024], F32)
    qhi = sbuf(st_A2, "qhi", [128, 1024], I32)
    qhm = sbuf(st_A2, "qhm", [128, 1024], F32)
    xn_tm = sbuf(st_A2, "xn_tm", [128, D], BF16)
    xq_tm = xn_tm
    xn_fm = sbuf(st_A2, "xn_fm", [128, D], BF16)
    xq_fm = sbuf(st_A2, "xq_fm", [128, D], BF16)
    g_sb = sbuf(st_A2, "g_sb", [128, INNER], BF16)
    u_sb = sbuf(st_A2, "u_sb", [128, INNER], BF16)
    hu_sb = sbuf(st_A2, "hu_sb", [128, INNER], BF16)
    hq_q = sbuf(st_A2, "hq_q", [128, 1024], BF16)
    gam = sbuf(st_A2, "gam", [128, 1], F32)
    vp_sb = sbuf(st_A2, "vp_sb", [128, 240], BF16)
    bx_sb = sbuf(st_A2, "bx_sb", [128, 16], F32)
    a_tm = sbuf(st_A2, "a_tm", [128, 512], BF16)
    pvec = sbuf(st_A2, "pvec", [128, 32], F32)
    sstab = sbuf(st_A2, "sstab", [128, 2], F32)
    mv_scr = sbuf(st_A2, "mv_scr", [128, 256], F32)

    p_v = psum(st_A2, "p_v", [128, 512], F32)
    p_g = [psum(st_A2, f"p_g{i}", [128, 512], F32) for i in range(4)]

    f_a0 = P.op(V, lambda v: v.memset(a_tm[:], 0.0), deps=prologue_done)
    swd_r = sbuf(st_A2, "swd_r", [128, 1], F32)
    f_swr = P.op(V, lambda v: v.reciprocal(out=swd_r[:], in_=sw["d"][:]),
                 deps=[fin_d[-1]])

    apit = a_tm[:].ap[0][0]
    vpit = vp_sb[:].ap[0][0]
    mpit = mv_scr[:].ap[0][0]
    ppit = pvec[:].ap[0][0]
    rpit = c_rc[:].ap[0][0]

    slot_x = [None, None]
    prev_fxq = [None]
    prev_fya = [None]
    prev_qh = [None, None]
    pending_q = [None]
    prev_fh = [None]
    prev_mxg = [None]

    hq_free = None
    psg_free = [None, None, None, None]
    ptr_free = {0: None, 1: None}
    prev_xnT = None
    prev_xqT = None
    prev_vb = None
    prev_mx = None
    a_fhq = []
    for c in range(NCH):
        b = c % 2
        fx = P.dma(G, "xg%d" % b, sx[b][:], x_in[c * 128:(c + 1) * 128, :],
                   deps=[slot_x[b] if slot_x[b] is not None else pre_done[-1]])
        fxn = P.op(V, lambda v, b=b, c=c: v.tensor_scalar(
            out=xn_tm[:], in0=sx[b][:], scalar1=tabs["rstd"][:, c:c + 1],
            scalar2=tabs["beta"][:, c:c + 1], op0=ALU.mult, op1=ALU.add),
            deps=[fx, pre_done[c], prev_xnT])
        prevq = [prev_fxq[0]] if prev_fxq[0] else []
        fyq = P.op(V, lambda v, b=b, c=c: v.tensor_scalar(
            out=qy[:], in0=sx[b][:], scalar1=tabs["alpha"][:, c:c + 1],
            scalar2=tabs["betaq"][:, c:c + 1], op0=ALU.mult, op1=ALU.add),
            deps=[fx, pre_done[c]] + prevq)
        slot_x[b] = fyq
        fi = P.op(V, lambda v: v.tensor_copy(qi[:], qy[:]), deps=[fyq])
        fm = P.op(V, lambda v: v.tensor_tensor(out=qm[:], in0=qy[:], in1=qi[:],
                                               op=ALU.subtract), deps=[fi])
        fu = P.op(V, lambda v: v.scalar_tensor_tensor(
            out=qy[:], in0=qm[:], scalar=1.0, in1=qi[:], op0=ALU.is_ge, op1=ALU.add),
            deps=[fm])
        fu = P.op(V, lambda v: v.scalar_tensor_tensor(
            out=qy[:], in0=qm[:], scalar=-1.0, in1=qy[:],
            op0=ALU.is_lt, op1=ALU.subtract), deps=[fu])
        fxq_dep = [fu, prev_xqT]

        ft = None
        for k in range(8):
            ft = P.op(T, lambda t, k=k: t.transpose(
                p_tra[:, k * 128:(k + 1) * 128], xn_tm[:, k * 128:(k + 1) * 128],
                c_ident[:]),
                deps=[fxn, f_ident] + ([ptr_free[0]] if k == 0 else []))
        fe_xn = P.op(A, lambda s: s.copy(xn_fm[:], p_tra[:, 0:1024]), deps=[ft])
        ptr_free[0] = fe_xn
        fxq = P.op(V, lambda v: v.scalar_tensor_tensor(
            out=xq_tm[:], in0=qm[:], scalar=0.0, in1=qy[:],
            op0=ALU.is_lt, op1=ALU.add), deps=fxq_dep + [ft])
        prev_fxq[0] = fxq
        for k in range(8):
            ft = P.op(T, lambda t, k=k: t.transpose(
                p_trb[:, k * 128:(k + 1) * 128], xq_tm[:, k * 128:(k + 1) * 128],
                c_ident[:]),
                deps=[fxq] + ([ptr_free[1]] if k == 0 else []))
        fe_xq = P.op(A, lambda s: s.copy(xq_fm[:], p_trb[:, 0:1024]), deps=[ft])
        ptr_free[1] = fe_xq
        prev_xnT = ft
        ev_xn = [fe_xn] * 8
        ev_xq = [fe_xq] * 8
        prev_xqT = fe_xq

        if pending_q[0] is not None:
            pending_q[0]()
            pending_q[0] = None

        fb0 = P.op(T, lambda t: t.matmul(p_v[:, 0:256], c_ones1b[:], c_pbrow[:],
                                         start=True, stop=False),
                   deps=[f_ones1b, f_pb, prev_vb])
        fvm = fb0
        for k in range(8):
            fvm = P.op(T, lambda t, k=k: t.matmul(
                p_v[:, 0:256], xn_fm[:, k * 128:(k + 1) * 128],
                c_pw[:, k * 256:(k + 1) * 256], start=False, stop=(k == 7)),
                deps=[ev_xn[k]] + ([f_pw] if k == 0 else []))
        fve = P.op(A, lambda a: a.copy(vp_sb[:], p_v[:, 0:240]), deps=[fvm])
        fbe = P.op(A, lambda a: a.copy(bx_sb[:], p_v[:, 240:256]),
                   deps=[fvm, prev_mx])
        prev_vb = fbe
        fbxd = P.dma(S, "mxs", d_bx[c * 128:(c + 1) * 128, :], bx_sb[:], deps=[fbe])
        vpU = apx(vp_sb, 0, [[16, 15], [1, 16]])
        aU = apx(a_tm, 1, [[1, 15], [17, 16]])
        aL = apx(a_tm, 16, [[16, 15], [17, 16]])
        fa1 = P.op(G, lambda g: g.tensor_copy(aU, vpU), deps=[fve, f_a0, prev_mx])
        fa2 = P.op(G, lambda g: g.tensor_scalar(
            out=aL, in0=vpU, scalar1=-1.0, scalar2=None, op0=ALU.mult), deps=[fa1])
        in0A = AP(a_tm[:].tensor, 0, [[apit, 128], [16, 16], [1, 16]])
        fs3 = fs4 = None
        fprev = [fa2, f_rc]
        for it in range(4):
            if it == 0:
                in1p = AP(c_rc[:].tensor, c * 16, [[rpit, 128], [0, 16], [1, 16]])
            else:
                in1p = AP(pvec[:].tensor, ((it - 1) % 2) * 16,
                          [[ppit, 128], [0, 16], [1, 16]])
            dst = pvec[:, (it % 2) * 16:(it % 2) * 16 + 16]
            fmul = P.op(V, lambda v, in1p=in1p: v.tensor_tensor(
                out=mv_scr[:], in0=in0A, in1=in1p, op=ALU.mult), deps=fprev)
            mv3 = AP(mv_scr[:].tensor, 0, [[mpit, 128], [16, 16], [1, 16]])
            fred = P.op(V, lambda v, dst=dst: v.tensor_reduce(
                out=dst, in_=mv3, axis=AX.X, op=ALU.add), deps=[fmul])
            if it >= 2:
                fsm = P.op(V, lambda v, dst=dst: v.tensor_tensor(
                    out=mv_scr[:, 0:16], in0=dst, in1=dst, op=ALU.mult), deps=[fred])
                fss_ = P.op(V, lambda v, it=it: v.tensor_reduce(
                    out=sstab[:, it - 2:it - 1], in_=mv_scr[:, 0:16], axis=AX.X,
                    op=ALU.add), deps=[fsm, prev_mx])
                if it == 2:
                    fs3 = fss_
                else:
                    fs4 = fss_
            fprev = [fred]
        fssd = P.dma(S, "mxs", d_ss[c * 128:(c + 1) * 128, :], sstab[:],
                     deps=[fs3, fs4])
        fad = P.dma(S, "mxs", d_abar[c * 128:(c + 1) * 128, :], a_tm[:, 0:256],
                    deps=[fa2, fs4])
        prev_mx = fad

        ep_g = [None] * 8
        ep_u = [None] * 8
        for grp in range(4):
            slices = list(range(grp * 4, grp * 4 + 4))
            mm_last = [None] * 4
            for k in range(8):
                for j, sl in enumerate(slices):
                    if sl < 8:
                        w_ap = wg_t[:, k * 4096 + sl * 512: k * 4096 + sl * 512 + 512]
                        wdep = fin_g[k]
                    else:
                        w_ap = wu_t[:, k * 4096 + (sl - 8) * 512:
                                    k * 4096 + (sl - 8) * 512 + 512]
                        wdep = fin_u[k]
                    deps = []
                    if j == 0:
                        deps.append(ev_xq[k])
                    if k == 0:
                        deps.append(wdep)
                        deps.append(psg_free[j])
                    mm_last[j] = P.op(T, lambda t, k=k, j=j, w=w_ap: t.matmul(
                        p_g[j][:, 0:512], xq_fm[:, k * 128:(k + 1) * 128], w,
                        start=(k == 0), stop=(k == 7)), deps=deps)
            for j, sl in enumerate(slices):
                if sl < 8:
                    fep = P.op(A, lambda s, j=j, sl=sl, c=c: s.activation(
                        out=g_sb[:, sl * 512:(sl + 1) * 512], in_=p_g[j][:, 0:512],
                        func=AF.Silu, scale=tabs["cg"][:, c:c + 1]),
                        deps=[mm_last[j], prev_fh[0]])
                    ep_g[sl] = fep
                    psg_free[j] = fep
                else:
                    su = sl - 8
                    feu = P.op(A, lambda s, j=j, su=su: s.copy(
                        u_sb[:, su * 512:(su + 1) * 512], p_g[j][:, 0:512]),
                        deps=[mm_last[j], prev_fh[0]])
                    ep_u[su] = feu
                    psg_free[j] = feu
        def emit_hquant(c=c, ep_g=ep_g, ep_u=ep_u):
            nonlocal hq_free
            fh_full = P.op(V, lambda v: v.tensor_tensor(
                out=hu_sb[:], in0=g_sb[:], in1=u_sb[:], op=ALU.mult),
                deps=ep_g + ep_u + [prev_fya[0]])
            prev_fh[0] = fh_full
            fgm = P.op(V, lambda v: v.tensor_reduce(
                out=gam[:], in_=hu_sb[:], axis=AX.X, op=ALU.max,
                apply_absolute_value=True), deps=[fh_full])
            prev_mxg[0] = fgm
            fg1 = P.op(V, lambda v, c=c: v.tensor_tensor(
                out=gam[:], in0=gam[:], in1=tabs["cu"][:, c:c + 1], op=ALU.mult),
                deps=[fgm])
            fg2 = P.op(V, lambda v: v.tensor_scalar(
                out=gam[:], in0=gam[:], scalar1=1e-5, scalar2=None, op0=ALU.max),
                deps=[fg1])
            fg4 = P.op(V, lambda v, c=c: v.scalar_tensor_tensor(
                out=tabs["cd"][:, c:c + 1], in0=swd_r[:], scalar=1.0 / 127.0,
                in1=gam[:], op0=ALU.mult, op1=ALU.mult), deps=[fg2, f_swr])
            fg7 = P.op(V, lambda v: v.reciprocal(out=t1c[:], in_=gam[:]),
                       deps=[fg4])
            fg8 = P.op(V, lambda v, c=c: v.scalar_tensor_tensor(
                out=gam[:], in0=t1c[:], scalar=127.0, in1=tabs["cu"][:, c:c + 1],
                op0=ALU.mult, op1=ALU.mult), deps=[fg7])
            fq_dma = None
            fqa_p, fma_p = prev_qh
            fua_p = fqa_p
            for hh in range(4):
                hsl = slice(hh * 1024, (hh + 1) * 1024)
                fya = P.op(V, lambda v, hsl=hsl, fg8=fg8, fh_full=fh_full:
                    v.tensor_scalar(
                        out=qhy[:], in0=hu_sb[:, hsl], scalar1=gam[:, 0:1],
                        scalar2=0.5, op0=ALU.mult, op1=ALU.add),
                    deps=[fg8, fh_full, fqa_p, fma_p])
                prev_fya[0] = fya
                fia = P.op(V, lambda v: v.tensor_copy(qhi[:], qhy[:]),
                           deps=[fya, fua_p])
                fma = P.op(V, lambda v: v.tensor_tensor(
                    out=qhm[:], in0=qhy[:], in1=qhi[:], op=ALU.subtract),
                    deps=[fia, fqa_p])
                fma_p = fma
                fua = P.op(V, lambda v: v.scalar_tensor_tensor(
                    out=qhy[:], in0=qhm[:], scalar=1.0, in1=qhi[:],
                    op0=ALU.is_ge, op1=ALU.add), deps=[fma])
                fua_p = fua
                fub = P.op(V, lambda v: v.scalar_tensor_tensor(
                    out=qhy[:], in0=qhm[:], scalar=-1.0, in1=qhy[:],
                    op0=ALU.is_lt, op1=ALU.subtract), deps=[fua])
                fqa = P.op(V, lambda v: v.scalar_tensor_tensor(
                    out=hq_q[:], in0=qhm[:], scalar=0.0, in1=qhy[:],
                    op0=ALU.is_lt, op1=ALU.add),
                    deps=[fub, hq_free if hh == 0 else fq_dma])
                fqa_p = fqa
                fq_dma = P.dma(S, "hqs",
                               d_hq[c * 128:(c + 1) * 128,
                                    hh * 1024:(hh + 1) * 1024],
                               hq_q[:], deps=[fqa])
            prev_qh[0], prev_qh[1] = fqa_p, fma_p
            hq_free = fq_dma
            a_fhq.append(fq_dma)
        pending_q[0] = emit_hquant

    if pending_q[0] is not None:
        pending_q[0]()
        pending_q[0] = None

    st_A2.close()
    st_A.close()
    barrier_A = P.barrier()

    st_M = ExitStack()
    a_ck = sbuf(st_M, "a_ck", [128, NCH * 256], BF16)
    b_ck = sbuf(st_M, "b_ck", [128, CS * 16], F32)
    ss_ck = sbuf(st_M, "ss_ck", [128, CS * 2], F32)
    s_ck = sbuf(st_M, "s_ck", [128, CS], F32)
    w_ck = sbuf(st_M, "w_ck", [128, CS * 256], BF16)
    bsq = sbuf(st_M, "bsq", [128, 256], BF16)
    e_t = sbuf(st_M, "e_t", [128, 256], BF16)
    f_t = sbuf(st_M, "f_t", [128, 256], BF16)
    prod = sbuf(st_M, "prod", [128, 8192], BF16)
    pfx = sbuf(st_M, "pfx", [128, 4352], BF16)
    xcar = sbuf(st_M, "xcar", [128, 272], F32)
    h0ck = sbuf(st_M, "h0ck", [128, 16], F32)
    hall = sbuf(st_M, "hall", [128, CS * 16], F32)
    comb = sbuf(st_M, "comb", [2, 8 * 272], F32)
    combh = sbuf(st_M, "combh", [2, 48], F32)
    combs = sbuf(st_M, "combs", [2, 256], F32)
    h0sq = sbuf(st_M, "h0sq", [2, 64 * 16], F32)

    apck = a_ck[:].ap[0][0]
    spck = s_ck[:].ap[0][0]
    ppck = prod[:].ap[0][0]
    wpck = w_ck[:].ap[0][0]
    xpit = xcar[:].ap[0][0]
    fpit = pfx[:].ap[0][0]
    cpit = comb[:].ap[0][0]
    hpit = combh[:].ap[0][0]
    h0pit = h0ck[:].ap[0][0]
    ept = e_t[:].ap[0][0]
    fpt = f_t[:].ap[0][0]

    P.dma(S, "msc", a_ck[:], AP(d_abar.tensor, 0, [[CS * 256, 128], [1, CS * 256]]),
          deps=barrier_A)
    P.dma(S, "msc", b_ck[:], AP(d_bx.tensor, 0, [[CS * 16, 128], [1, CS * 16]]),
          deps=barrier_A)
    fl_ss = P.dma(S, "msc", ss_ck[:], AP(d_ss.tensor, 0, [[CS * 2, 128], [1, CS * 2]]),
                  deps=barrier_A)
    fl_a = fl_b = fl_ss

    f_s1 = P.op(V, lambda v: v.tensor_scalar(
        out=s_ck[:], in0=apx(ss_ck, 0, [[2, CS]]), scalar1=1e-30, scalar2=None,
        op0=ALU.max), deps=[fl_ss, barrier_A[0]])
    f_s2 = P.op(V, lambda v: v.reciprocal(out=s_ck[:], in_=s_ck[:]), deps=[f_s1])
    f_s3 = P.op(V, lambda v: v.tensor_tensor(
        out=s_ck[:], in0=s_ck[:], in1=apx(ss_ck, 1, [[2, CS]]), op=ALU.mult),
        deps=[f_s2])
    f_s4 = P.op(V, lambda v: v.tensor_scalar(
        out=s_ck[:], in0=s_ck[:], scalar1=1.0 / 0.09, scalar2=1.0,
        op0=ALU.mult, op1=ALU.max), deps=[f_s3])
    f_s5 = P.op(A, lambda s: s.activation(out=s_ck[:], in_=s_ck[:], func=AF.Ln),
                deps=[f_s4])
    f_s6 = P.op(A, lambda s: s.activation(out=s_ck[:], in_=s_ck[:], func=AF.Exp,
                                          scale=-0.5), deps=[f_s5])
    f_sc = P.op(V, lambda v: v.tensor_tensor(
        out=AP(a_ck[:].tensor, 0, [[apck, 128], [256, CS], [1, 256]]),
        in0=AP(a_ck[:].tensor, 0, [[apck, 128], [256, CS], [1, 256]]),
        in1=AP(s_ck[:].tensor, 0, [[spck, 128], [1, CS], [0, 256]]),
        op=ALU.mult), deps=[fl_a, f_s6])

    st_Bd = ExitStack()
    wd_t = sbuf(st_Bd, "wd_t", [128, 32 * 1024], BF16)
    shq = sbuf(st_Bd, "shq", [128, INNER], BF16)
    hq_fm = sbuf(st_Bd, "hq_fm", [128, 32 * 128], BF16)
    yd_sb = sbuf(st_Bd, "yd_sb", [128, D], F32)
    xn2 = sbuf(st_Bd, "xn2", [128, D], BF16)
    xnd = yd_sb
    po = [sbuf(st_Bd, f"po{i}", [128, D], F32) for i in range(2)]
    c_dtile = sbuf(st_Bd, "c_dtile", [128, D], F32)
    p_d = [psum(st_Bd, f"p_d{i}", [128, 512], F32) for i in range(4)]

    f_dtile = P.dma(S, "bc1", c_dtile[:], drow.partition_broadcast(128),
                    deps=barrier_A)
    fl_wd = []
    for k in range(32):
        prev_same = [fl_wd[k - 4]] if k >= 4 else barrier_A
        fl_wd.append(P.dma(S if k % 2 == 0 else A, "wdl%d" % (k % 4),
                           wd_t[:, k * 1024:(k + 1) * 1024],
                           d_wdt[k * 128:(k + 1) * 128, :],
                           deps=prev_same))

    slotB_hq = [None]
    slotB_x = list(slot_x)
    slotB_po = [None, None]
    ptrB = dict(ptr_free)
    pdB = [None, None, None, None]
    prevB = {"t1": None, "t2": None, "xnd": None}
    bdown_last = [None]

    btail = [None]

    def emit_bdown(c):
        b = c % 2
        if btail[0] is not None:
            btail[0]()
            btail[0] = None
        fhl = P.dma(S, "hqb", shq[:], d_hq[c * 128:(c + 1) * 128, :],
                    deps=[a_fhq[c], slotB_hq[0]])
        fxl = P.dma(S, "xa%d" % b, sx[b][:], x_in[c * 128:(c + 1) * 128, :],
                    deps=[slotB_x[b]])
        ev4 = []
        ft = None
        for q in range(4):
            pt = p_tra if q % 2 == 0 else p_trb
            for kk in range(8):
                k = q * 8 + kk
                ft = P.op(T, lambda t, k=k, kk=kk, pt=pt: t.transpose(
                    pt[:, kk * 128:(kk + 1) * 128],
                    shq[:, k * 128:(k + 1) * 128], c_ident[:]),
                    deps=[fhl, f_ident] + ([ptrB[q % 2]] if kk == 0 else []))
            fe = P.op(A, lambda a, q=q, pt=pt: a.copy(
                hq_fm[:, q * 1024:(q + 1) * 1024], pt[:, 0:1024]), deps=[ft])
            ptrB[q % 2] = fe
            ev4.append(fe)
        slotB_hq[0] = ft
        pda = (c % 2) * 2
        mmd = [None, None]
        for k in range(32):
            for n in range(2):
                deps = []
                if k % 8 == 0 and n == 0:
                    deps.append(ev4[k // 8])
                if n == 0:
                    deps.append(fl_wd[k])
                if k == 0:
                    deps.append(pdB[pda + n])
                mmd[n] = P.op(T, lambda t, k=k, n=n, pda=pda: t.matmul(
                    p_d[pda + n][:, 0:512], hq_fm[:, k * 128:(k + 1) * 128],
                    wd_t[:, k * 1024 + n * 512: k * 1024 + n * 512 + 512],
                    start=(k == 0), stop=(k == 31)), deps=deps)
        fyd = []
        for n in range(2):
            f_ = P.op(A, lambda a, n=n, pda=pda, c=c: a.activation(
                out=yd_sb[:, n * 512:(n + 1) * 512], in_=p_d[pda + n][:, 0:512],
                func=AF.Identity, scale=tabs["cd"][:, c:c + 1]),
                deps=[mmd[n], prevB["t2"]])
            fyd.append(f_)
            pdB[pda + n] = f_
        fxn2 = P.op(A, lambda a, b=b, c=c: a.activation(
            out=xn2[:], in_=sx[b][:], func=AF.Identity,
            bias=tabs["beta"][:, c:c + 1], scale=tabs["rstd"][:, c:c + 1]),
            deps=[fxl, prevB["xnd"]])

        def emit_btail(c=c, b=b, fxl=fxl, fyd=fyd, fxn2=fxn2):
            ft1 = P.op(G, lambda g, b=b: g.tensor_tensor(
                out=po[b][:], in0=sx[b][:], in1=yd_sb[:], op=ALU.add),
                deps=[fxl, fyd[0], fyd[1], slotB_po[b]])
            slotB_x[b] = ft1
            fxnd = P.op(V, lambda v: v.tensor_tensor(
                out=xnd[:], in0=xn2[:], in1=c_dtile[:], op=ALU.mult),
                deps=[fxn2, f_dtile, ft1])
            ft2 = P.op(G, lambda g, b=b: g.tensor_tensor(
                out=po[b][:], in0=po[b][:], in1=xnd[:], op=ALU.add),
                deps=[ft1, fxnd])
            fpo = P.dma(S, "po%d" % b, d_part[c * 128:(c + 1) * 128, :], po[b][:],
                        deps=[ft2])
            slotB_po[b] = fpo
            prevB["t1"] = ft1
            prevB["xnd"] = fxnd
            prevB["t2"] = ft2
            bdown_last[0] = fpo
        btail[0] = emit_btail

    f_xc = P.op(V, lambda v: v.tensor_copy(xcar[:], c_xinit[:]),
                deps=[f_xinit] + barrier_A[0:1])
    fst_l = [None] * CS
    fw1_prev = None
    fba_l = [None] * CS
    fr1_l = [None] * (CS + 1)
    fw3_l = [None] * CS
    fp1_l = [None] * (CS + 1)

    def mk_fp1(s):
        base = s * 256
        return P.op(V, lambda v, base=base: v.tensor_tensor(
            out=AP(prod[:].tensor, 0, [[ppck, 128], [256, 16], [16, 16], [1, 16]]),
            in0=AP(a_ck[:].tensor, base, [[apck, 128], [16, 16], [0, 16], [1, 16]]),
            in1=AP(a_ck[:].tensor, base, [[apck, 128], [0, 16], [16, 16], [1, 16]]),
            op=ALU.mult),
            deps=[f_sc, fr1_l[s - 1] if s > 0 else None,
                  barrier_A[4] if s == 0 else None])

    def mk_pp(s):
        return P.op(G, lambda g, s=s: g.tensor_tensor(
            out=AP(pfx[:].tensor, 0, [[fpit, 128], [256, 17], [16, 16], [1, 16]]),
            in0=AP(xcar[:].tensor, 0, [[xpit, 128], [16, 17], [0, 16], [1, 16]]),
            in1=AP(w_ck[:].tensor, s * 256, [[wpck, 128], [0, 17], [16, 16], [1, 16]]),
            op=ALU.mult),
            deps=[fba_l[s - 1] if s > 0 else f_xc, fw3_l[s]])

    def mk_rr(s, fpp):
        frr = P.op(V, lambda v: v.tensor_reduce(
            out=AP(xcar[:].tensor, 0, [[xpit, 128], [16, 17], [1, 16]]),
            in_=AP(pfx[:].tensor, 0, [[fpit, 128], [16, 272], [1, 16]]),
            axis=AX.X, op=ALU.add),
            deps=[fpp, fst_l[s - 1] if s > 0 else None])
        fba = P.op(V, lambda v, s=s: v.tensor_tensor(
            out=xcar[:, 256:272], in0=xcar[:, 256:272],
            in1=b_ck[:, s * 16:(s + 1) * 16], op=ALU.add), deps=[frr, fl_b])
        fba_l[s] = fba

    def mk_st(s):
        fst_l[s] = P.dma(S, "xst",
                         AP(d_xsto.tensor, s * 272, [[CS * 272, 128], [1, 272]]),
                         xcar[:], deps=[fba_l[s]])

    fp1_l[0] = mk_fp1(0)
    fpp_p = None
    for s in range(CS):
        base = s * 256
        if s > 0:
            fpp_p = mk_pp(s - 1)
        fr1 = P.op(V, lambda v: v.tensor_reduce(
            out=bsq[:], in_=AP(prod[:].tensor, 0,
                               [[ppck, 128], [16, 256], [1, 16]]),
            axis=AX.X, op=ALU.add), deps=[fp1_l[s], fw1_prev])
        fr1_l[s] = fr1
        if s + 1 < CS:
            fp1_l[s + 1] = mk_fp1(s + 1)
        fe_ = P.op(V, lambda v, base=base: v.tensor_tensor(
            out=e_t[:], in0=bsq[:], in1=a_ck[:, base:base + 256], op=ALU.subtract),
            deps=[fr1])
        ff_ = P.op(V, lambda v, base=base: v.tensor_tensor(
            out=f_t[:], in0=bsq[:], in1=a_ck[:, base:base + 256], op=ALU.add),
            deps=[fe_])
        fp2 = P.op(V, lambda v: v.tensor_tensor(
            out=AP(prod[:].tensor, 4096, [[ppck, 128], [256, 16], [16, 16], [1, 16]]),
            in0=AP(f_t[:].tensor, 0, [[fpt, 128], [16, 16], [0, 16], [1, 16]]),
            in1=AP(e_t[:].tensor, 0, [[ept, 128], [0, 16], [16, 16], [1, 16]]),
            op=ALU.mult), deps=[fe_, ff_])
        fr2 = P.op(V, lambda v, s=s: v.tensor_reduce(
            out=w_ck[:, s * 256:(s + 1) * 256],
            in_=AP(prod[:].tensor, 4096, [[ppck, 128], [16, 256], [1, 16]]),
            axis=AX.X, op=ALU.add), deps=[fp2])
        if s > 0:
            mk_rr(s - 1, fpp_p)
        fw1a = P.op(V, lambda v, base=base: v.tensor_scalar(
            out=e_t[:], in0=a_ck[:, base:base + 256], scalar1=-2.0,
            scalar2=None, op0=ALU.mult), deps=[fr2, fp2])
        fw1 = P.op(V, lambda v: v.tensor_tensor(
            out=e_t[:], in0=e_t[:], in1=bsq[:], op=ALU.subtract), deps=[fw1a])
        fw1_prev = fw1
        if s > 0:
            mk_st(s - 1)
        fw2 = P.op(V, lambda v, s=s: v.tensor_tensor(
            out=w_ck[:, s * 256:(s + 1) * 256], in0=w_ck[:, s * 256:(s + 1) * 256],
            in1=e_t[:], op=ALU.add), deps=[fw1])
        fw3 = P.op(V, lambda v, s=s: v.tensor_tensor(
            out=w_ck[:, s * 256:(s + 1) * 256], in0=w_ck[:, s * 256:(s + 1) * 256],
            in1=c_itile[:], op=ALU.add), deps=[fw2, f_itile])
        fw3_l[s] = fw3
        emit_bdown(s)
    fpp_p = mk_pp(CS - 1)
    mk_rr(CS - 1, fpp_p)
    mk_st(CS - 1)
    if btail[0] is not None:
        btail[0]()
        btail[0] = None
    fstep = fst_l[CS - 1]

    f_xcd = P.dma(S, "msc", d_xc, xcar[:], deps=[fstep])
    f_h00 = P.op(V, lambda v: v.memset(combh[:], 0.0), deps=[f_xcd])
    f_h0s = P.op(V, lambda v: v.memset(h0sq[:], 0.0), deps=[f_h00])
    fprevh = f_h0s
    f_cmb = None
    for j in range(64):
        if j % 8 == 0:
            f_cmb = P.dma(S, "msc", comb[:],
                          AP(d_xc.tensor, j * 272,
                             [[64 * 272, 2], [1, 8 * 272]]),
                          deps=[f_xcd, fprevh])
        jj = j % 8
        cur = (j % 2) * 16
        nxt = ((j + 1) % 2) * 16
        fmv = P.op(V, lambda v, jj=jj, cur=cur: v.tensor_tensor(
            out=AP(combs[:].tensor, 0, [[combs[:].ap[0][0], 2], [1, 16], [16, 16]]),
            in0=AP(comb[:].tensor, jj * 272, [[cpit, 2], [1, 16], [16, 16]]),
            in1=AP(combh[:].tensor, cur, [[hpit, 2], [0, 16], [1, 16]]),
            op=ALU.mult), deps=[fprevh, f_cmb])
        frv = P.op(V, lambda v: v.tensor_reduce(
            out=AP(combh[:].tensor, 32, [[hpit, 2], [1, 16]]),
            in_=AP(combs[:].tensor, 0, [[combs[:].ap[0][0], 2], [1, 16], [16, 16]]),
            axis=AX.X, op=ALU.add), deps=[fmv])
        fad2 = P.op(V, lambda v, jj=jj, nxt=nxt: v.tensor_tensor(
            out=AP(combh[:].tensor, nxt, [[hpit, 2], [1, 16]]),
            in0=AP(combh[:].tensor, 32, [[hpit, 2], [1, 16]]),
            in1=AP(comb[:].tensor, jj * 272 + 256, [[cpit, 2], [1, 16]]),
            op=ALU.add), deps=[frv])
        if j < 63:
            fprevh = P.op(V, lambda v, j=j, nxt=nxt: v.tensor_copy(
                h0sq[:, (j + 1) * 16:(j + 2) * 16],
                AP(combh[:].tensor, nxt, [[hpit, 2], [1, 16]])), deps=[fad2])
        else:
            fprevh = fad2
    f_h0d = P.dma(S, "msc", AP(d_h0.tensor, 0, [[64 * 16, 2], [1, 64 * 16]]),
                  h0sq[:], deps=[fprevh])
    f_h0l = P.dma(S, "msc", h0ck[:], d_h0, deps=[f_h0d])

    st_Bd.close()

    st_C = ExitStack()
    xsto_f = sbuf(st_C, "xsto_f", [128, CS * 272], F32)
    h_tmc = sbuf(st_C, "h_tmc", [128, 16], BF16)
    h_fmT = sbuf(st_C, "h_fmT", [16, 128], BF16)
    pin = [sbuf(st_C, f"pin{i}", [128, D], F32) for i in range(2)]
    so = [sbuf(st_C, f"so{i}", [128, D], F32) for i in range(2)]
    c_cw = sbuf(st_C, "c_cw", [16, D], BF16)
    p_y = psum(st_C, "p_y", [128, 1024], F32)
    xspit = xsto_f[:].ap[0][0]

    f_cw = P.dma(G, "bc2", c_cw[:], cwT_in, deps=barrier_A)
    fxsl = P.dma(S, "xsl", xsto_f[:], d_xsto,
                 deps=[fst_l[CS - 1], bdown_last[0], slotB_po[0], slotB_po[1],
                       pdB[0], pdB[1], pdB[2], pdB[3], ptrB[0], ptrB[1],
                       slotB_hq[0], prevB["xnd"], prevB["t2"]])
    fre1 = P.op(G, lambda g: g.tensor_tensor(
        out=AP(prod[:].tensor, 0, [[ppck, 128], [256, 32], [16, 16], [1, 16]]),
        in0=AP(xsto_f[:].tensor, 0, [[xspit, 128], [272, 32], [1, 16], [16, 16]]),
        in1=AP(h0ck[:].tensor, 0, [[h0pit, 128], [0, 32], [0, 16], [1, 16]]),
        op=ALU.mult), deps=[f_h0l, fxsl])
    fre2 = P.op(V, lambda v: v.tensor_reduce(
        out=hall[:], in_=AP(prod[:].tensor, 0,
                            [[ppck, 128], [256, 32], [16, 16], [1, 16]]),
        axis=AX.X, op=ALU.add), deps=[fre1])
    fre3 = P.op(V, lambda v: v.tensor_tensor(
        out=hall[:], in0=hall[:],
        in1=AP(xsto_f[:].tensor, 256, [[xspit, 128], [272, 32], [1, 16]]),
        op=ALU.add), deps=[fre2])
    f_htm = P.dma(G, "htms", AP(d_htm.tensor, 0, [[CS * 16, 128], [1, CS * 16]]),
                  hall[:], deps=[fre3])

    slotC_pin = [None, None]
    slotC_out = [None, None]
    prev_fth = None
    feh_p = ptrB[1]
    pyC = None
    for c in range(NCH):
        b = c % 2
        fh2 = P.dma(S, "htm2", h_tmc[:],
                    AP(d_htm.tensor, c * 2048, [[16, 128], [1, 16]]),
                    deps=[f_htm if c == 0 else prev_fth])
        fth = P.op(T, lambda t: t.transpose(p_trb[0:16, 0:128], h_tmc[:],
                                            c_ident[:]), deps=[fh2, feh_p])
        prev_fth = fth
        feh = P.op(A, lambda a: a.copy(h_fmT[:], p_trb[0:16, 0:128]), deps=[fth])
        feh_p = feh
        fy1 = P.op(T, lambda t: t.matmul(p_y[:, 0:512], h_fmT[:], c_cw[:, 0:512],
                                         start=True, stop=True),
                   deps=[feh, f_cw, pyC])
        fy2 = P.op(T, lambda t: t.matmul(p_y[:, 512:1024], h_fmT[:],
                                         c_cw[:, 512:1024], start=True, stop=True),
                   deps=[fy1])
        fpl = P.dma(A, "pi%d" % b, pin[b][:], d_part[c * 128:(c + 1) * 128, :],
                    deps=[slotB_po[b] if c < 2 else None, slotC_pin[b]])
        fo = P.op(V, lambda v, b=b: v.tensor_tensor(
            out=so[b][:], in0=pin[b][:], in1=p_y[:, 0:1024], op=ALU.add),
            deps=[fy2, fpl, slotC_out[b]])
        pyC = fo
        slotC_pin[b] = fo
        fod = P.dma(S, "out%d" % b, y_out[c * 128:(c + 1) * 128, :], so[b][:],
                    deps=[fo])
        slotC_out[b] = fod

    st_C.close()
    st_M.close()
    fin = P.barrier()
    P.op(S, lambda e: e.nop(), deps=fin, inc=0)

    st_sem = ExitStack()
    sems = {e: st_sem.enter_context(nc.semaphore("sem_" + e))
            for e in list(ENGINES) + sorted(P.streams)}
    with nc.Block() as block:
        with nc.allow_low_precision(reason="quantized ternary/int8 data is exact in bf16"):
            P.emit(block, sems)
    st_sem.close()
    st_top.close()
    return nc


_CACHE = {}


def _rconst():
    if "r" not in _CACHE:
        import jax
        with jax.default_device(jax.devices("cpu")[0]):
            key = jax.random.key(1, impl="threefry2x32")
            r = np.asarray(jax.random.normal(key, (16, 2048, 16, 1),
                                             jax.numpy.float32))[..., 0]
        jax.config.update("jax_enable_x64", True)
        with jax.default_device(jax.devices("cpu")[0]):
            key = jax.random.key(1, impl="threefry2x32")
            r64 = np.asarray(jax.random.normal(key, (16, 2048, 16, 1),
                                               jax.numpy.float64))[..., 0]
        jax.config.update("jax_enable_x64", False)
        _CACHE["r"] = r
        _CACHE["r64"] = r64
    return _CACHE["r"]


def make_in_maps(L0_attn_u, L0_attn_W_proj_w, L0_attn_W_proj_b, L0_attn_B_w,
                 L0_attn_C_w, L0_attn_D, L0_ffn_w_gate, L0_ffn_w_up, L0_ffn_w_down):
    r = _rconst()
    f32 = np.float32
    wgT = np.ascontiguousarray(np.asarray(L0_ffn_w_gate, f32).T)
    wuT = np.ascontiguousarray(np.asarray(L0_ffn_w_up, f32).T)
    wdT = np.ascontiguousarray(np.asarray(L0_ffn_w_down, f32).T)
    cwT = np.ascontiguousarray(np.asarray(L0_attn_C_w, f32).T)
    Wp = np.asarray(L0_attn_W_proj_w, f32)
    Wpb = np.asarray(L0_attn_W_proj_b, f32)
    Bw = np.asarray(L0_attn_B_w, f32)
    Dv = np.asarray(L0_attn_D, f32)
    x = np.asarray(L0_attn_u, f32)

    rows, cols = np.triu_indices(16, 1)
    pidx = {(int(a), int(bb)): k for k, (a, bb) in enumerate(zip(rows, cols))}
    pw_cols = np.zeros((256, 1024), f32)
    pb_row = np.zeros((1, 256), f32)
    for d in range(1, 16):
        for i in range(16 - d):
            k = pidx[(i, i + d)]
            pw_cols[(d - 1) * 16 + i] = Wp[k]
            pb_row[0, (d - 1) * 16 + i] = Wpb[k]
    pw_cols[240:256] = Bw
    pw_host = np.ascontiguousarray(pw_cols.T)

    ident = np.eye(128, dtype=BF)
    ones1b = np.ones((1, 128), dtype=BF)
    ones1f = np.ones((1, 128), dtype=f32)
    onescol = np.ones((1, 1), dtype=f32)
    irow = np.zeros((1, 256), dtype=BF)
    for i in range(16):
        irow[0, i * 17] = BF(1.0)
    xinit = np.zeros((1, 272), f32)
    for i in range(16):
        xinit[0, i * 16 + i] = 1.0
    drow_host = np.ascontiguousarray(Dv.reshape(1, D))

    r64 = _CACHE["r64"]

    def _arrange(rr):
        return np.ascontiguousarray(
            rr.reshape(NCH, 128, 16).transpose(1, 0, 2).reshape(128, NCH * 16))

    in_maps = []
    for core in range(NCORES):
        xs = np.ascontiguousarray(x[2 * core:2 * core + 2].reshape(TOK, D))
        rc = _arrange(r[2 * core:2 * core + 2].reshape(TOK, 16).astype(f32))
        r64s = r64[2 * core:2 * core + 2].reshape(TOK, 16)
        rbh = r64s.astype(f32)
        rbl = (r64s - rbh.astype(np.float64)).astype(f32)
        in_maps.append(dict(
            x=xs, wgT=wgT, wuT=wuT, wdT=wdT, pw=pw_host, pbrow=pb_row,
            cwT=cwT, drow=drow_host, rc=rc, rbh=_arrange(rbh), rbl=_arrange(rbl),
            ident=ident, ones1b=ones1b,
            ones1f=ones1f, onescol=onescol, irow=irow, xinit=xinit,
        ))
    return in_maps


def kernel(L0_attn_u, L0_attn_W_proj_w, L0_attn_W_proj_b, L0_attn_B_w, L0_attn_C_w,
           L0_attn_D, L0_ffn_w_gate, L0_ffn_w_up, L0_ffn_w_down):
    in_maps = make_in_maps(L0_attn_u, L0_attn_W_proj_w, L0_attn_W_proj_b,
                           L0_attn_B_w, L0_attn_C_w, L0_attn_D, L0_ffn_w_gate,
                           L0_ffn_w_up, L0_ffn_w_down)
    nc = _CACHE.get("nc")
    if nc is None:
        nc = build_program()
        _CACHE["nc"] = nc
    res = run_bass_kernel_spmd(nc, in_maps, core_ids=list(range(NCORES)))
    out = np.empty((16, 2048, D), np.float32)
    for core in range(NCORES):
        out[2 * core:2 * core + 2] = res.results[core]["y_out"].reshape(2, 2048, D)
    return out
